# version 33
# baseline (speedup 1.0000x reference)
import numpy as np
import ml_dtypes
import concourse.bass as bass
import concourse.mybir as mybir
from concourse.bass_utils import run_bass_kernel_spmd

BF16 = mybir.dt.bfloat16
F32 = mybir.dt.float32
I32 = mybir.dt.int32
AF = mybir.ActivationFunctionType
ALU = mybir.AluOpType
AX = mybir.AxisListType

B, T, D, NH, HD = 16, 2048, 1024, 16, 64
NCORE = 8
BS = B // NCORE
CH = 128
NCH = T // CH
DUO = 256
NDUO = T // DUO
HIDDEN = 3072
NHT = HIDDEN // 128
EPS = 1e-5
NPAIR = 8
QS = 1.0 / 127.0
GN_N = float(D * T)

BFNP = ml_dtypes.bfloat16


def _flat(waits):
    out = []
    for w in waits:
        if w is None:
            continue
        if isinstance(w, (list, tuple)) and w and isinstance(w[0], (list, tuple, type(None))):
            out.extend(_flat(w))
        elif isinstance(w, list):
            out.extend(_flat(w))
        else:
            out.append(w)
    return out


NLANE = 4
STAGES = ["proj", "kv", "scores", "inner", "nocross", "nozd", "full"]


class Sched:

    def __init__(self, nc, names, queues, auto_chain=("dv", "ac")):
        self.nc = nc
        self.e = {}
        self.queues = set(queues)
        self.auto_chain = set(auto_chain)
        self.sems = {}
        for name in list(names) + list(queues):
            self.e[name] = dict(n=0, ops=[], seen={})
        self.lane_n = {q: [0] * NLANE for q in queues}
        self.lane_i = {q: 0 for q in queues}
        self.pinned = {"gp": (3,)}
        self.po_n = 0

    def set_sems(self, sems):
        self.sems = sems

    def _add(self, eng, fn, waits, incs, force_waits=()):
        rec = self.e[eng]
        ws = [tuple(ev) for ev in _flat(force_waits)]
        for ev in _flat(waits):
            ename, val = ev
            if rec["seen"].get(ename, -1) >= val:
                continue
            rec["seen"][ename] = val
            ws.append((ename, val))
        rec["ops"].append((fn, ws, incs))

    def op(self, eng, fn, waits=(), inc=0, force_waits=(), lane=None):
        if eng in self.queues:
            assert inc == 16
            if lane is None:
                lane = self.lane_i[eng]
                self.lane_i[eng] = (lane + 1) % NLANE
                if self.lane_i[eng] in self.pinned.get(eng, ()):
                    self.lane_i[eng] = (self.lane_i[eng] + 1) % NLANE
            lname = f"{eng}{lane}"
            prev = self.lane_n[eng][lane]
            ws = list(waits) + [(lname, prev)]
            self.lane_n[eng][lane] = prev + 16
            self._add(eng, fn, ws, [(lname, 16)], force_waits=force_waits)
            return (lname, prev + 16)
        if eng in self.auto_chain:
            waits = list(waits) + [(eng, self.e[eng]["n"])]
            inc = 1
        elif inc:
            waits = list(waits) + [(eng, self.e[eng]["n"])]
        self.e[eng]["n"] += inc
        self._add(eng, fn, waits, [(eng, inc)] if inc else [],
                  force_waits=force_waits)
        return (eng, self.e[eng]["n"]) if inc else None

    def cop(self, fn, waits=(), force_waits=()):
        ws = list(waits) + [("po", self.po_n)]
        self.po_n += 1
        self._add("gp", fn, ws, [("po", 1)], force_waits=force_waits)
        return ("po", self.po_n)

    def ev(self, eng):
        if eng == "po":
            return ("po", self.po_n)
        if eng in self.queues:
            return [(f"{eng}{l}", self.lane_n[eng][l]) for l in range(NLANE)]
        return (eng, self.e[eng]["n"])

    def emit(self, eng_name, engine):
        for fn, ws, incs in self.e[eng_name]["ops"]:
            for ename, val in ws:
                engine.wait_ge(self.sems[ename], val)
            inst = fn(engine)
            for iname, ival in incs:
                inst.then_inc(self.sems[iname], ival)


class Builder:
    def __init__(self, expose=(), n_samples=BS, n_duos=NDUO, with_b=True,
                 with_ret=True, stage="full", proj_evac="full"):
        self.proj_evac = proj_evac
        self.n_samples = n_samples
        self.n_duos = n_duos
        self.with_b = with_b
        self.with_ret = with_ret
        self.stage = stage
        self.expose = expose
        nc = bass.Bass("TRN2", target_bir_lowering=False)
        self.nc = nc
        self.S = Sched(nc, ["pe", "dv", "ac"], queues=["sy", "gp"])

        di = lambda name, shape, dtype: nc.dram_tensor(
            name, shape, dtype, kind="ExternalInput").ap()

        def dsc(name, shape, dtype):
            kind = "ExternalOutput" if name in expose else "Internal"
            return nc.dram_tensor(name, shape, dtype, kind=kind).ap()

        self.x_in = di("x", [BS, T, D], F32)
        self.xT_in = di("xT", [BS, D, T], F32)
        self.wqkv_in = di("wqkv", [D, 3 * D], F32)
        self.wgT_in = di("wgT", [D, HIDDEN], I32)
        self.wuT_in = di("wuT", [D, HIDDEN], I32)
        self.wdT_in = di("wdT", [HIDDEN, D], I32)
        self.p_in = di("pmat", [128, NPAIR * HD], F32)
        self.gnw_in = di("gnw", [128, NPAIR], F32)
        self.gnb_in = di("gnb", [128, NPAIR], F32)
        self.gqf_in = di("gqf", [128, NPAIR * CH], BF16)
        self.gkf_in = di("gkf", [128, NPAIR * CH], BF16)
        self.g0q_in = di("g0q", [128, NCH * NPAIR], F32)
        self.g0k_in = di("g0k", [128, NCH * NPAIR], F32)
        self.gkt_in = di("gkt", [NCH, 128, D], BF16)
        self.mask_in = di("mask", [128, 2 * CH], BF16)
        self.ident_in = di("ident", [128, 128], F32)

        self.out_d = nc.dram_tensor("outT", [BS, D, T], F32,
                                    kind="ExternalOutput").ap()
        self.xs_d = dsc("xs_sc", [BS, 8, 128, T], BF16)
        self.r_d = dsc("r_sc", [BS, 8, 128, T], BF16)

        sb = nc.alloc_sbuf_tensor
        self.wg_a = sb("wg_a", [128, 8 * HIDDEN], BF16)
        self.wu_a = sb("wu_a", [128, 8 * HIDDEN], BF16)
        self.wd_a = sb("wd_a", [128, 8 * HIDDEN], BF16)

        self.ident = sb("identS", [128, 128], F32)
        self.mask2 = sb("mask2", [128, 2 * CH], BF16)
        self.p_bf = sb("p_bf", [128, NPAIR * HD], BF16)
        self.gnw_c = sb("gnw_c", [128, NPAIR], F32)
        self.gnb_c = sb("gnb_c", [128, NPAIR], F32)
        self.gqf = sb("gqfS", [128, NPAIR * CH], BF16)
        self.gkf = sb("gkfS", [128, NPAIR * CH], BF16)
        self.g0q = sb("g0qS", [128, NCH * NPAIR], F32)
        self.g0k = sb("g0kS", [128, NCH * NPAIR], F32)
        self.ones_row = sb("ones_row", [1, 128], BF16)
        self.ones_col = sb("ones_col", [128, 1], BF16)
        self.eps_col = sb("eps_col", [128, 1], F32)
        self.z_f32 = sb("z_f32a", [128, 512], F32)
        self.z_bf = sb("z_bfS", [128, 512], BF16)
        self.acol = [sb("acol0", [128, NPAIR], F32), sb("acol1", [128, NPAIR], F32)]
        self.bcol = [sb("bcol0", [128, NPAIR], F32), sb("bcol1", [128, NPAIR], F32)]

        qkq = sb("qkq", [128, 3 * NPAIR * DUO], BF16)
        self.qp_b = qkq[:, 0:2048]
        self.kp_b = qkq[:, 2048:4096]
        self.qr_b = qkq[:, 4096:6144]
        self.hid_b = qkq[:]

        xsd0 = sb("xsd0", [128, 8 * DUO], BF16)
        xsd1 = sb("xsd1", [128, 8 * DUO], BF16)
        self.xs_duo = [xsd0[:], xsd1[:]]
        self.g_sup = xsd0[:]
        gkt2 = sb("gkt2", [128, 2 * D], BF16)
        self.gkt_b = [gkt2[:, 0:D], gkt2[:, D:2 * D]]
        self.xs_sup = [xsd1[:], gkt2[:]]
        xst = sb("xstok", [128, 2 * D], BF16)
        self.xs_tok = [xst[:, 0:D], xst[:, D:2 * D]]
        ktv = sb("ktv", [128, 2 * D], BF16)
        ktvb = sb("ktvb", [128, 2 * D], BF16)
        self.ktok = [ktv[:, 0:D], ktvb[:, 0:D]]
        self.vtok = [ktv[:, D:2 * D], ktvb[:, D:2 * D]]
        self.r_sup = [xst[:], ktv[:]]

        f32a = sb("f32a", [128, 2816], F32)
        self.x_tok = [f32a[:, 0:D], f32a[:, 1792:1792 + D]]
        self.p_f32 = f32a[:, D:D + 512]
        self.rsum_c = f32a[:, 1536:1536 + NCH * NPAIR]
        self.rsq_c = f32a[:, 1664:1664 + NCH * NPAIR]
        self.x_sup = f32a[:, 0:2048]
        self.out_b = [f32a[:, 2048:2048 + DUO], f32a[:, 2304:2304 + DUO]]

        self.identb = sb("identB", [128, 128], BF16)
        self.at_p = [sb("at_p0", [128, 2 * CH], BF16), sb("at_p1", [128, 2 * CH], BF16)]
        self.r_pair = [sb("r_pair0", [128, CH], BF16), sb("r_pair1", [128, CH], BF16)]
        self.sq_scr = sb("sq_scr", [128, D], BF16)
        self.sq_scr2 = sb("sq_scr2", [128, CH], BF16)
        self.stat_c = [sb("stat_c0", [128, 8], F32), sb("stat_c1", [128, 8], F32)]
        self.rows2 = sb("rows2", [1, 2 * CH], BF16)
        self.stats2 = sb("stats2", [128, 2], F32)
        self.stats2b = sb("stats2b", [128, 2], BF16)
        self.tot_sb = sb("tot_sb", [2, 1], F32)
        self.trow = sb("trow", [1, 2], F32)
        self.sc_sb = sb("sc_sb", [1, 8], F32)
        self.vals2 = sb("vals2", [1, 2], BF16)
        self.bc2 = sb("bc2", [128, 2], F32)

        self.silu_b = self.sq_scr[:, 0:DUO]
        self.s_tile = self.sq_scr[:, DUO:2 * DUO]
        self.sg_tile = self.sq_scr[:, 2 * DUO:3 * DUO]

        self.pb = [nc.alloc_psum_tensor(f"pb{i}", [128, 512], F32) for i in range(8)]
        self.tp_ps = self.pb[7][0:1, 256:384]
        self.tot_ps = self.pb[7][0:2, 384:385]
        self.tr2_ps = self.pb[7][0:1, 386:388]
        self.bc2_ps = self.pb[7][:, 388:390]

        self._gn_tail = None
        self.bank_free = [None] * 8
        self.ev_ktv_use = [None, None]
        self.ev_xsduo_dma = [None, None]
        self.ev_zbf = None
        self.ev_zuse = None
        self.ev_xsduo_w = {}
        self.ev_rchunk_w = {}
        self.ev_acols = [None, None]
        self.ev_wqkv_lastuse = None
        self.ev_gktuse = [None, None]
        self.ev_gkt_ev = [None, None]
        self.ev_xstok_use = [None, None]
        self.ev_xtok_use = [None, None]
        self.ev_xt_dma = [None, None]
        self.ev_statc_use = [None, None]
        self.ev_qkbuf_use = None
        self.ev_xsduo_use = [None, None]
        self.ev_at_use = [None, None]
        self.ev_rpair_use = [None, None]
        self.ev_tmp_use = [None, None]
        self.ev_gn_done = None

    def wqkv_t(self, d_i, c0, w):
        return self.wd_a[:, d_i * 3 * D + c0: d_i * 3 * D + c0 + w]

    def wg_t(self, d_i, h0, w):
        return self.wg_a[:, d_i * HIDDEN + h0: d_i * HIDDEN + h0 + w]

    def wu_t(self, d_i, h0, w):
        return self.wu_a[:, d_i * HIDDEN + h0: d_i * HIDDEN + h0 + w]

    def wd_t(self, h_t, c0, w):
        return self.wd_a[:, h_t * D + c0: h_t * D + c0 + w]

    def setup(self):
        S = self.S
        self.ev_const = {}
        self.ev_const["ident"] = S.op(
            "sy", lambda e: e.dma_start(out=self.ident[:], in_=self.ident_in),
            inc=16)
        self.ln_stats_dma(0, 0)
        self.ln_stats_dma(0, 1)
        for key, dst, src in (
            ("mask", self.mask2, self.mask_in),
            ("pmat", self.p_f32, self.p_in), ("gnw", self.gnw_c, self.gnw_in),
            ("gnb", self.gnb_c, self.gnb_in), ("gqf", self.gqf, self.gqf_in),
            ("gkf", self.gkf, self.gkf_in), ("g0q", self.g0q, self.g0q_in),
            ("g0k", self.g0k, self.g0k_in),
        ):
            self.ev_const[key] = S.op(
                "sy", lambda e, d=dst, s=src: e.dma_start(out=d[:], in_=s), inc=16)

        self.ev_wqkv = S.op("gp", lambda e: e.dma_start(
            out=self.wd_a[:].rearrange("p (a c) -> p a c", a=8),
            in_=self.wqkv_in.rearrange("(a p) c -> p a c", p=128)), inc=16)

        self.ev_identb = S.op("dv", lambda e: e.tensor_copy(
            self.identb[:], self.ident[:]), waits=[self.ev_const["ident"]], inc=1)
        NWARM = 430
        for i in range(NWARM):
            S.op("pe", lambda e, b=i % 2: e.matmul(
                self.pb[b][:, 0:128], self.identb[:], self.identb[:],
                start=True, stop=True),
                waits=[self.ev_identb] if i == 0 else (),
                inc=1 if i == NWARM - 1 else 0)
        self.bank_free[0] = S.ev("pe")
        self.bank_free[1] = S.ev("pe")
        self.ev_ones = S.op("dv", lambda e: e.memset(self.ones_row[:], 1.0), inc=1)
        self.ev_onesc = S.op("dv", lambda e: e.memset(self.ones_col[:], 1.0), inc=1)
        self.ev_eps = S.op("dv", lambda e: e.memset(self.eps_col[:], EPS), inc=1)
        self.ev_z0 = S.op("dv", lambda e: e.memset(self.z_f32[:], 0.0), inc=1)
        self.ev_pbf = S.op("dv", lambda e: e.tensor_scalar_mul(
            self.p_bf[:], self.p_f32[:], float(NH * HD)),
            waits=[self.ev_const["pmat"]], inc=1)

    def ln_stats_dma(self, s, c):
        par = c % 2
        self.ev_xt_dma[par] = self.S.op("sy", lambda e, par=par: e.dma_start(
            out=self.x_tok[par][:], in_=self.x_in[s, c * CH:(c + 1) * CH, :]),
            waits=[self.ev_xtok_use[par]], inc=16)

    def ln_stats_pre_items(self, s, c, box, key):
        S = self.S
        par = c % 2
        st = self.stat_c[par]
        ev_xt = self.ev_xt_dma[par]
        use0 = self.ev_statc_use[par]

        def _rs():
            ev = S.op("dv", lambda e, par=par: e.reduce_sum(
                st[:, 0:1], self.x_tok[par][:], axis=AX.X),
                waits=[ev_xt, use0], inc=1)
            self.ev_xtok_use[par] = [ev]
            box[(key, 'rs')] = ev

        def _sq():
            ev = S.op("ac", lambda e, par=par: e.activation(
                self.sq_scr[:], self.x_tok[par][:], AF.Square,
                accum_out=st[:, 1:2]),
                waits=[ev_xt, use0], inc=1)
            self.ev_xtok_use[par].append(ev)

        def _m2():
            S.op("ac", lambda e: e.mul(st[:, 2:3], st[:, 0:1], 1.0 / D),
                 waits=[box[(key, 'rs')]])

        def _m3():
            S.op("ac", lambda e: e.mul(st[:, 3:4], st[:, 1:2], -1.0 / D))

        def _m4():
            S.op("ac", lambda e: e.mul(st[:, 4:5], st[:, 2:3], st[:, 2:3]))

        def _a4():
            S.op("ac", lambda e: e.add(st[:, 4:5], st[:, 4:5], st[:, 3:4]))

        def _sqrt():
            S.op("ac", lambda e: e.activation(st[:, 5:6], st[:, 4:5], AF.Sqrt,
                                              scale=-1.0, bias=self.eps_col[:]),
                 waits=[self.ev_eps], inc=1)
            box[key] = S.ev("ac")

        return [_rs, _sq, _m2, _m3, _m4, _a4, _sqrt]

    def ln_stats_pre(self, s, c):
        box = {}
        for f in self.ln_stats_pre_items(s, c, box, 'k'):
            f()
        return box['k']

    def ln_stats_fin(self, s, c, ev_std):
        S = self.S
        par = c % 2
        st = self.stat_c[par]
        ev_rec = S.op("dv", lambda e: e.reciprocal(st[:, 6:7], st[:, 5:6]),
                      waits=[ev_std], inc=1)
        S.op("dv", lambda e: e.scalar_tensor_tensor(
            st[:, 7:8], st[:, 2:3], -1.0, st[:, 6:7], ALU.mult, ALU.mult),
            force_waits=[ev_rec], inc=1)
        ev_xst = S.op("dv", lambda e, par=par: e.tensor_scalar(
            self.xs_tok[par][:], self.x_tok[par][:], st[:, 6:7], st[:, 7:8],
            ALU.mult, ALU.add),
            waits=[self.ev_xstok_use[par]], inc=1)
        self.ev_statc_use[par] = ev_xst
        self.ev_xtok_use[par] = self.ev_xtok_use[par] + [ev_xst]
        return ev_xst

    def xsT_group(self, s, c, half, bank, ev_xst):
        S = self.S
        duo, par = c // 2, c % 2
        dbuf = self.xs_duo[duo % 2]
        evs_t = []
        for i in range(4):
            d_i = half * 4 + i
            evs_t.append(S.op("pe", lambda e, i=i, d_i=d_i, bank=bank: e.transpose(
                self.pb[bank][:, i * 64:(i + 1) * 64].bitcast(BF16),
                self.xs_tok[par][:, d_i * 128:(d_i + 1) * 128],
                self.identb[:]),
                waits=[ev_xst, self.ev_identb, self.bank_free[bank]]
                if i == 0 else (), inc=1))
        first_w = [self.ev_xsduo_use[duo % 2], self.ev_xsduo_dma[duo % 2]] \
            if (par == 0 and half == 0) else []
        for i in range(4):
            d_i = half * 4 + i
            S.op("ac", lambda e, i=i, d_i=d_i, bank=bank, par=par: e.copy(
                dbuf[:, d_i * DUO + par * CH: d_i * DUO + (par + 1) * CH],
                self.pb[bank][:, i * 64:(i + 1) * 64].bitcast(BF16)),
                waits=[evs_t[i]] + first_w, inc=1)
        ev = S.ev("ac")
        self.bank_free[bank] = ev
        self.ev_xstok_use[par] = evs_t[-1]
        return ev

    def qk_proj(self, s, duo, ev_xs, drip=None):
        S = self.S
        drip = list(drip) if drip else []
        dbuf = self.xs_duo[duo % 2]
        for rnd in range(8):
            if rnd > 0:
                for _ in range(2):
                    if drip:
                        drip.pop(0)()
            bnk = rnd % 2
            is_q = rnd < 4
            p0 = (rnd % 4) * 2
            col0 = 0 if is_q else D
            for pi in range(2):
                p = p0 + pi
                for d_i in range(8):
                    S.op("pe", lambda e, p=p, pi=pi, d_i=d_i, bnk=bnk, col0=col0: e.matmul(
                        self.pb[bnk][:, pi * DUO:(pi + 1) * DUO],
                        self.wqkv_t(d_i, col0 + p * 128, 128),
                        dbuf[:, d_i * DUO:(d_i + 1) * DUO],
                        start=(d_i == 0), stop=(d_i == 7)),
                        waits=[ev_xs, self.ev_wqkv, self.bank_free[bnk]]
                        if (d_i == 0 and pi == 0) else (),
                        inc=1 if (d_i == 7 and pi == 1) else 0)
            ev_mm = S.ev("pe")
            self.ev_wqkv_lastuse = ev_mm
            if self.proj_evac == "none":
                ev_dv = S.op("dv", lambda e, bnk=bnk, is_q=is_q: e.tensor_copy(
                    (self.qp_b if is_q else self.kp_b)[:, 0:512],
                    self.pb[bnk][:]),
                    waits=[ev_mm, self.ev_qkbuf_use], inc=1)
                self.bank_free[bnk] = ev_dv
                continue
            n_dv = 0
            for pi in range(2):
                p = p0 + pi
                for cc in range(2):
                    cg = duo * 2 + cc
                    src = self.pb[bnk][:, pi * DUO + cc * CH: pi * DUO + (cc + 1) * CH]
                    n_dv += 1
                    last_dv = n_dv == 4
                    if self.proj_evac in ("copy", "copyac"):
                        S.op("dv", lambda e, p=p, cc=cc, src=src, is_q=is_q: e.tensor_copy(
                            (self.qp_b if is_q else self.kp_b)[
                                :, p * DUO + cc * CH: p * DUO + (cc + 1) * CH], src),
                            waits=[ev_mm, self.ev_qkbuf_use], inc=1 if last_dv else 0)
                    elif is_q:
                        S.op("dv", lambda e, p=p, cc=cc, cg=cg, src=src: e.scalar_tensor_tensor(
                            self.qp_b[:, p * DUO + cc * CH: p * DUO + (cc + 1) * CH],
                            src, self.g0q[:, cg * NPAIR + p: cg * NPAIR + p + 1],
                            self.gqf[:, p * CH:(p + 1) * CH], ALU.mult, ALU.mult),
                            waits=[ev_mm, self.ev_qkbuf_use, self.ev_const["g0q"],
                                   self.ev_const["gqf"]],
                            inc=1 if last_dv else 0)
                    else:
                        S.op("dv", lambda e, p=p, cc=cc, cg=cg, src=src: e.scalar_tensor_tensor(
                            self.kp_b[:, p * DUO + cc * CH: p * DUO + (cc + 1) * CH],
                            src, self.g0k[:, cg * NPAIR + p: cg * NPAIR + p + 1],
                            self.gkf[:, p * CH:(p + 1) * CH], ALU.mult, ALU.mult),
                            waits=[ev_mm, self.ev_qkbuf_use, self.ev_const["g0k"],
                                   self.ev_const["gkf"]],
                            inc=1 if last_dv else 0)
            ev_dv = S.ev("dv")
            ev_ac = None
            if is_q and self.proj_evac in ("full", "copyac"):
                for pi in range(2):
                    p = p0 + pi
                    S.op("ac", lambda e, p=p, pi=pi, bnk=bnk: e.copy(
                        self.qr_b[:, p * DUO:(p + 1) * DUO],
                        self.pb[bnk][:, pi * DUO:(pi + 1) * DUO]),
                        waits=[ev_mm, self.ev_qkbuf_use, ev_dv],
                        inc=1 if pi == 1 else 0)
                ev_ac = S.ev("ac")
            self.bank_free[bnk] = [ev_dv, ev_ac]
        ev_ret = [S.ev("dv"), S.ev("ac")]
        for f in drip:
            f()
        return ev_ret

    def ktv_gkt_load(self, c):
        par = c % 2
        self.ev_gkt_ev[par] = self.S.op("sy", lambda e: e.dma_start(
            out=self.gkt_b[par][:], in_=self.gkt_in[c]),
            waits=[self.ev_gktuse[par]], inc=16)

    def ktv_group(self, s, c, grp, bank, ev_xs):
        S = self.S
        duo, par = c // 2, c % 2
        dbuf = self.xs_duo[duo % 2]
        is_k = grp < 2
        half = grp % 2
        col0 = (D if is_k else 2 * D) + half * 512
        for d_i in range(8):
            S.op("pe", lambda e, d_i=d_i, bank=bank, col0=col0: e.matmul(
                self.pb[bank][:, 0:512],
                dbuf[:, d_i * DUO + par * CH: d_i * DUO + (par + 1) * CH],
                self.wqkv_t(d_i, col0, 512),
                start=(d_i == 0), stop=(d_i == 7)),
                waits=[ev_xs, self.ev_wqkv, self.bank_free[bank]] if d_i == 0 else (),
                inc=1 if d_i == 7 else 0)
        ev_mm = S.ev("pe")
        cols = slice(half * 512, (half + 1) * 512)
        if is_k:
            ev = S.op("dv", lambda e, bank=bank, cols=cols, par=par: e.tensor_tensor(
                self.ktok[par][:, cols], self.pb[bank][:, 0:512],
                self.gkt_b[par][:, cols], ALU.mult),
                waits=[ev_mm, self.ev_gkt_ev[par], self.ev_ktv_use[par]], inc=1)
            self.ev_gktuse[par] = ev
        else:
            ev = S.op("ac", lambda e, bank=bank, cols=cols, par=par: e.copy(
                self.vtok[par][:, cols], self.pb[bank][:, 0:512]),
                waits=[ev_mm, self.ev_ktv_use[par]], inc=1)
        self.bank_free[bank] = ev
        return ev

    def retention(self, s, c, ev_kev, ev_vev, ev_qk, mid_cbs=None,
                  skip_zd=False):
        S = self.S
        lvl = STAGES.index(self.stage)
        do_inner = lvl >= 3
        do_merge = lvl >= 4
        do_cross = lvl >= 5
        do_zd = lvl >= 6 and not skip_zd
        duo, par = c // 2, c % 2
        ccol = par * CH
        ktok, vtok = self.ktok[par], self.vtok[par]
        mid_cbs = mid_cbs or {}

        ev_mask = [None] * NPAIR

        def front(p):
            pp = p % 2
            sb_bank = self.pb[pp]
            qcols = slice(p * DUO + ccol, p * DUO + ccol + CH)
            S.op("pe", lambda e, sb_bank=sb_bank, qcols=qcols: e.matmul(
                sb_bank[:, 0:CH], self.kp_b[0:64, qcols], self.qp_b[0:64, qcols],
                start=True, stop=True),
                waits=[ev_qk, self.bank_free[pp]])
            ev_sc = S.op("pe", lambda e, sb_bank=sb_bank, qcols=qcols: e.matmul(
                sb_bank[:, CH:2 * CH], self.kp_b[64:128, qcols],
                self.qp_b[64:128, qcols], start=True, stop=True), inc=1)

            ev_mk = S.op("dv", lambda e, pp=pp, sb_bank=sb_bank: e.tensor_tensor(
                self.at_p[pp][:], sb_bank[:, 0:2 * CH], self.mask2[:], ALU.mult),
                waits=[ev_sc, self.ev_at_use[pp], self.ev_const["mask"]], inc=1)
            ev_mask[p] = ev_mk
            self.bank_free[pp] = ev_mk

        def back(p):
            pp = p % 2
            ci_bank = self.pb[2 + pp]
            qcols = slice(p * DUO + ccol, p * DUO + ccol + CH)
            if not do_inner and not do_cross:
                return
            first_waits = [self.bank_free[2 + pp], ev_mask[p], ev_vev]
            if do_cross:
                first_waits += [self.ev_pbf, ev_qk]
                if c > 0:
                    first_waits.append(self.ev_zbf)
            first = True
            for base in (0, 64):
                rows = slice(base, base + 64)
                vcols = slice(p * 128 + base, p * 128 + base + 64)
                acols = slice(0, CH) if base == 0 else slice(CH, 2 * CH)
                if do_cross:
                    if c > 0:
                        S.op("pe", lambda e, p=p, rows=rows, ci_bank=ci_bank, qcols=qcols: e.matmul(
                            ci_bank[rows, CH:2 * CH], self.z_bf[rows, p * 64:(p + 1) * 64],
                            self.qp_b[rows, qcols], start=True, stop=False,
                            skip_group_check=True),
                            waits=first_waits if first else (),
                            inc=1 if (base == 64 and p == NPAIR - 1) else 0)
                        first = False
                        S.op("pe", lambda e, p=p, rows=rows, ci_bank=ci_bank, qcols=qcols: e.matmul(
                            ci_bank[rows, CH:2 * CH], self.p_bf[rows, p * 64:(p + 1) * 64],
                            self.qr_b[rows, qcols], start=False, stop=False,
                            skip_group_check=True))
                    else:
                        S.op("pe", lambda e, p=p, rows=rows, ci_bank=ci_bank, qcols=qcols: e.matmul(
                            ci_bank[rows, CH:2 * CH], self.p_bf[rows, p * 64:(p + 1) * 64],
                            self.qr_b[rows, qcols], start=True, stop=False,
                            skip_group_check=True),
                            waits=first_waits if first else ())
                        first = False
                if do_inner:
                    S.op("pe", lambda e, rows=rows, vcols=vcols, acols=acols,
                         ci_bank=ci_bank, st=not do_cross: e.matmul(
                        ci_bank[rows, CH:2 * CH], vtok[:, vcols],
                        self.at_p[pp][:, acols], start=st, stop=True,
                        skip_group_check=True),
                        waits=first_waits if first else (),
                        inc=1 if base == 64 else 0)
                    first = False
            if do_cross and c > 0 and p == NPAIR - 1:
                self.ev_zuse = S.ev("pe")
            if not do_inner:
                return
            ev_inner = S.ev("pe")
            self.ev_at_use[pp] = ev_inner

            if not do_merge:
                return
            col = c * NPAIR + p
            ev_mg = S.op("dv", lambda e, pp=pp, ci_bank=ci_bank, col=col: e.tensor_scalar(
                self.r_pair[pp][:], ci_bank[:, CH:2 * CH], 1.0, 0.0,
                ALU.mult, ALU.add, accum_out=self.rsum_c[:, col:col + 1]),
                waits=[ev_inner, self.ev_rpair_use[pp], self.ev_gn_done], inc=1)
            self.bank_free[2 + pp] = ev_mg
            ev_sq = S.op("ac", lambda e, pp=pp, col=col: e.activation(
                self.sq_scr2[:], self.r_pair[pp][:], AF.Square,
                accum_out=self.rsq_c[:, col:col + 1]),
                waits=[ev_mg, self.ev_gn_done], inc=1)
            ev_rw = S.op("gp", lambda e, s=s, p=p, c=c, pp=pp: e.dma_start(
                out=self.r_d[s, p, :, c * CH:(c + 1) * CH], in_=self.r_pair[pp][:]),
                waits=[ev_mg], inc=16)
            self.ev_rpair_use[pp] = [ev_sq, ev_rw]
            if p == NPAIR - 1:
                self.ev_rchunk_w[(s, c)] = ev_rw

        def zdelta():
            first = True
            for p in range(NPAIR):
                for base in (0, 64):
                    rows = slice(base, base + 64)
                    koff = p * 128 + base
                    S.op("pe", lambda e, p=p, rows=rows, koff=koff: e.matmul(
                        self.pb[6][rows, p * 64:(p + 1) * 64],
                        ktok[:, koff:koff + 64], vtok[:, koff:koff + 64],
                        start=True, stop=True),
                        waits=[ev_kev, ev_vev, self.bank_free[6]] if first else (),
                        inc=1 if (p == NPAIR - 1 and base == 64) else 0)
                    first = False

        front(0)
        front(1)
        if do_zd:
            zdelta()
        for p in range(2, NPAIR):
            back(p - 2)
            if p in mid_cbs:
                mid_cbs[p]()
            front(p)
        if 'tail' in mid_cbs:
            mid_cbs['tail']()
        back(NPAIR - 2)
        back(NPAIR - 1)

        if do_zd:
            ev_zd = S.ev("pe")
            ev_zadd = S.op("dv", lambda e: e.tensor_tensor(
                self.z_f32[:], self.z_f32[:], self.pb[6][:], ALU.add),
                waits=[ev_zd, self.ev_z0], inc=1)
            self.bank_free[6] = ev_zadd
            self.ev_zbf = S.op("dv", lambda e: e.tensor_copy(
                self.z_bf[:], self.z_f32[:]),
                waits=[self.ev_zuse], inc=1)

        self.ev_qkbuf_use = S.ev("pe")

    def gn_finalize(self, s):
        self.gn_p1(s)
        self.gn_p2(s)
        self.gn_p3(s)
        self.gn_p4(s)

    def gn_p1(self, s):
        S = self.S
        ev_sums = [S.ev("dv"), S.ev("ac")]
        S.op("ac", lambda e: e.activation(
            self.p_f32[:, 0:128], self.rsum_c[:], AF.Copy,
            accum_out=self.stats2[:, 0:1]),
            waits=ev_sums)
        S.op("ac", lambda e: e.activation(
            self.p_f32[:, 128:256], self.rsq_c[:], AF.Copy,
            accum_out=self.stats2[:, 1:2]))
        S.op("ac", lambda e: e.copy(self.stats2b[:], self.stats2[:]), inc=1)
        ev1 = S.ev("ac")
        self.ev_gn_done = ev1
        self._gn_ev1 = ev1

    def gn_p2(self, s):
        S = self.S
        ev2 = S.op("pe", lambda e: e.matmul(
            self.tot_ps, self.stats2b[:], self.ones_col[:], start=True, stop=True),
            waits=[self._gn_ev1, self.ev_onesc, self.bank_free[7]], inc=1)
        self._gn_ev3 = S.op("ac", lambda e: e.copy(self.tot_sb[:], self.tot_ps),
                            waits=[ev2], inc=1)

    def gn_p3(self, s):
        S = self.S
        ev4 = S.op("pe", lambda e: e.transpose(
            self.tr2_ps, self.tot_sb[:], self.ident[0:2, 0:2]),
            waits=[self._gn_ev3], inc=1)
        ev5 = S.op("ac", lambda e: e.copy(self.trow[:], self.tr2_ps),
                   waits=[ev4], inc=1)
        sc = self.sc_sb
        S.op("ac", lambda e: e.mul(sc[0:1, 0:1], self.trow[0:1, 0:1], 1.0 / GN_N),
             waits=[ev5])
        S.op("ac", lambda e: e.mul(sc[0:1, 1:2], self.trow[0:1, 1:2], -1.0 / GN_N))
        S.op("ac", lambda e: e.mul(sc[0:1, 2:3], sc[0:1, 0:1], sc[0:1, 0:1]))
        S.op("ac", lambda e: e.add(sc[0:1, 2:3], sc[0:1, 2:3], sc[0:1, 1:2]))
        S.op("ac", lambda e: e.activation(sc[0:1, 6:7], sc[0:1, 2:3], AF.Sqrt,
                                          scale=-1.0, bias=self.eps_col[0:1, :]),
             waits=[self.ev_eps], inc=1)
        ev_gstd = S.ev("ac")
        ev_grec = S.op("dv", lambda e: e.reciprocal(sc[0:1, 3:4], sc[0:1, 6:7]),
                       waits=[ev_gstd], inc=1)
        S.op("dv", lambda e: e.scalar_tensor_tensor(
            sc[0:1, 5:6], sc[0:1, 0:1], -1.0, sc[0:1, 3:4], ALU.mult, ALU.mult),
            force_waits=[ev_grec])
        S.op("dv", lambda e: e.tensor_copy(self.vals2[0:1, 0:1], sc[0:1, 3:4]))
        S.op("dv", lambda e: e.tensor_copy(self.vals2[0:1, 1:2], sc[0:1, 5:6]), inc=1)
        self._gn_ev9 = S.ev("dv")

    def gn_p4(self, s):
        S = self.S
        ev10 = S.op("pe", lambda e: e.matmul(
            self.bc2_ps, self.ones_row[:], self.vals2[:], start=True, stop=True),
            waits=[self._gn_ev9, self.ev_ones], inc=1)
        ev11 = S.op("ac", lambda e: e.copy(self.bc2[:], self.bc2_ps),
                    waits=[ev10], inc=1)
        self.bank_free[7] = ev11
        S.op("ac", lambda e, s=s: e.mul(
            self.acol[s][:], self.gnw_c[:], self.bc2[:, 0:1]),
            waits=[ev11, self.ev_const["gnw"]], inc=1)
        ev_a = S.ev("ac")
        S.op("dv", lambda e, s=s: e.scalar_tensor_tensor(
            self.bcol[s][:], self.gnw_c[:], self.bc2[:, 1:2], self.gnb_c[:],
            ALU.mult, ALU.add), waits=[ev11, self.ev_const["gnb"]], inc=1)
        self.ev_acols[s] = [ev_a, S.ev("dv")]

    def xs_store(self, s, duo, ev):
        ev_w = self.S.op(
            "gp", lambda e, s=s, duo=duo: e.dma_start(
                out=self.xs_d[s, :, :, duo * DUO:(duo + 1) * DUO].rearrange(
                    "d p t -> p d t"),
                in_=self.xs_duo[duo % 2].rearrange("p (d t) -> p d t", d=8)),
            waits=[ev], inc=16)
        self.ev_xsduo_w[(s, duo)] = ev_w
        self.ev_xsduo_dma[duo % 2] = ev_w

    def stats_and_xs(self, s, duo, ev_cols2):
        self.xsT_group(s, 2 * duo, 0, 4, ev_cols2[0])
        a2 = self.xsT_group(s, 2 * duo, 1, 5, ev_cols2[0])
        self.xsT_group(s, 2 * duo + 1, 0, 4, ev_cols2[1])
        b2 = self.xsT_group(s, 2 * duo + 1, 1, 5, ev_cols2[1])
        self.xs_store(s, duo, b2)
        return [a2, b2]

    def phase_a(self):
        lvl = STAGES.index(self.stage)
        S = self.S
        self.wgu_parts = []
        total = self.n_samples * self.n_duos

        def chain_full(s, c, dma=True):
            if dma:
                self.ln_stats_dma(s, c)
            return self.ln_stats_fin(s, c, self.ln_stats_pre(s, c))

        if not self.with_ret:
            for s in range(self.n_samples):
                ev_cols2 = [chain_full(s, 0, dma=(s > 0)),
                            chain_full(s, 1, dma=(s > 0))]
                evs_xs = self.stats_and_xs(s, 0, ev_cols2)
                for duo in range(self.n_duos - 1):
                    nxt = [chain_full(s, 2 * duo + 2), chain_full(s, 2 * duo + 3)]
                    evs_xs = self.stats_and_xs(s, duo + 1, nxt)
            return

        ev_cols2 = [chain_full(0, 0, dma=False), chain_full(0, 1, dma=False)]
        evs_xs = self.stats_and_xs(0, 0, ev_cols2)
        for g in range(total):
            s, d = divmod(g, self.n_duos)
            c0, c1 = 2 * d, 2 * d + 1
            box = {}
            drip = []
            if g + 1 < total:
                s2, d2 = divmod(g + 1, self.n_duos)
                self.ln_stats_dma(s2, 2 * d2)
                self.ln_stats_dma(s2, 2 * d2 + 1)
                drip = (self.ln_stats_pre_items(s2, 2 * d2, box, 'a')
                        + self.ln_stats_pre_items(s2, 2 * d2 + 1, box, 'b'))
            ev_qk = self.qk_proj(s, d, evs_xs[-1], drip=drip)
            self.prefetch_wgu(s, d)
            evs_xs_cur = evs_xs
            nxt_box = [None, None]
            if lvl >= 1:
                self.ktv_gkt_load(c0)
                k0 = [self.ktv_group(s, c0, 0, 0, evs_xs_cur[0]),
                      self.ktv_group(s, c0, 1, 1, evs_xs_cur[0])]
                v0 = [self.ktv_group(s, c0, 2, 2, evs_xs_cur[0]),
                      self.ktv_group(s, c0, 3, 3, evs_xs_cur[0])]
            nxt_info = None
            if g + 1 < total:
                nxt_info = (s2, d2,
                            [self.ln_stats_fin(s2, 2 * d2, box['a']),
                             self.ln_stats_fin(s2, 2 * d2 + 1, box['b'])])
            gn_pend = None
            if d == 0 and s > 0:
                gn_pend = s - 1
                self.gn_p1(s - 1)
                ev = S.op("dv", lambda e: e.memset(self.z_f32[:], 0.0),
                          waits=[self.ev_zuse], inc=1)
                self.ev_z0 = ev
                self.ev_zbf = None
                self.ev_zuse = None
            if lvl < 1:
                if nxt_info is not None:
                    s2, d2, cols = nxt_info
                    evs_xs = self.stats_and_xs(s2, d2, cols)
                continue
            if lvl < 2:
                self.ktv_gkt_load(c1)
                for grp, bank in ((0, 4), (1, 4), (2, 5), (3, 5)):
                    self.ktv_group(s, c1, grp, bank, evs_xs_cur[1])
                self.ev_ktv_use = [S.ev("pe"), S.ev("pe")]
                if nxt_info is not None:
                    s2, d2, cols = nxt_info
                    evs_xs = self.stats_and_xs(s2, d2, cols)
                continue

            def mk_xs(par, half, bank, nxt_info=nxt_info, nxt_box=nxt_box):
                if nxt_info is None:
                    return
                s2, d2, cols = nxt_info
                ev = self.xsT_group(s2, 2 * d2 + par, half, bank, cols[par])
                if half == 1:
                    nxt_box[par] = ev
                    if par == 1:
                        self.xs_store(s2, d2, ev)

            self.ktv_gkt_load(c1)
            ev_c1 = {}
            cbs = {
                3: lambda: ev_c1.__setitem__(
                    'k0', self.ktv_group(s, c1, 0, 4, evs_xs_cur[1])),
                4: lambda: ev_c1.__setitem__(
                    'v0', self.ktv_group(s, c1, 2, 5, evs_xs_cur[1])),
                5: lambda: mk_xs(0, 0, 4),
                6: lambda: mk_xs(0, 1, 5),
                7: lambda: ev_c1.__setitem__(
                    'k1', self.ktv_group(s, c1, 1, 4, evs_xs_cur[1])),
                'tail': lambda: ev_c1.__setitem__(
                    'v1', self.ktv_group(s, c1, 3, 5, evs_xs_cur[1])),
            }
            self.retention(s, c0, k0, v0, ev_qk, mid_cbs=cbs)
            self.ev_ktv_use[0] = S.ev("pe")
            cbs1 = {3: lambda: mk_xs(1, 0, 4),
                    4: lambda: mk_xs(1, 1, 5)}
            if gn_pend is not None:
                cbs1[5] = lambda s0=gn_pend: self.gn_p2(s0)
                cbs1[6] = lambda s0=gn_pend: self.gn_p3(s0)
                cbs1[7] = lambda s0=gn_pend: self.gn_p4(s0)
            self.retention(s, c1, [ev_c1['k0'], ev_c1['k1']],
                           [ev_c1['v0'], ev_c1['v1']], ev_qk,
                           mid_cbs=cbs1,
                           skip_zd=(c1 == NCH - 1))
            self.ev_ktv_use[1] = S.ev("pe")
            self.ev_xsduo_use[d % 2] = S.ev("pe")
            evs_xs = nxt_box
        self._gn_tail = None
        if lvl >= 4:
            if self.with_b and self.stage == "full":
                self.gn_p1(self.n_samples - 1)
                self._gn_tail = self.n_samples - 1
            else:
                self.gn_finalize(self.n_samples - 1)

    def prefetch_wgu(self, s, duo):
        i = s * self.n_duos + duo
        if i >= 8:
            return
        for arena, src in ((self.wg_a, self.wgT_in), (self.wu_a, self.wuT_in)):
            self.wgu_parts.append(self.S.op(
                "gp", lambda e, arena=arena, src=src, i=i: e.dma_start(
                    out=arena[:, i * HIDDEN:(i + 1) * HIDDEN],
                    in_=src[i * 128:(i + 1) * 128, :]),
                inc=16, lane=3))

    def phase_b(self):
        S = self.S
        evA = [S.ev("pe"), S.ev("dv"), S.ev("ac"), S.ev("gp"), S.ev("sy")]
        ev_wd = S.op("gp", lambda e: e.dma_start(
            out=self.wd_a[:].rearrange("p (a c) -> p a c", a=24),
            in_=self.wdT_in.rearrange("(a p) c -> p a c", p=128)),
            waits=[self.ev_wqkv_lastuse, evA], inc=16)

        ev_xs_use = [None, None]
        ev_r_use = [None, None]
        ev_x_use = None
        ev_out_use = [None, None]
        ev_hid_use = None
        ev_g_use = None
        gate_free = [self.bank_free[0], self.bank_free[2]]
        up_free = [self.bank_free[1], self.bank_free[3]]
        down_free = [self.bank_free[4 + i] for i in range(4)]
        ev_silu_use = None

        duos = [(BS - 1, NDUO - 1, True)] + [
            (s, u, False) for s in range(BS) for u in range(NDUO)
            if not (s == BS - 1 and u == NDUO - 1)]
        first_g_extra = [S.ev("gp"), S.ev("sy")]
        for idx, (s, u, direct) in enumerate(duos):
                par = idx % 2
                if direct:
                    ev_xs = None
                else:
                    ev_xs = S.op("sy", lambda e, s=s, u=u, par=par: e.dma_start(
                        out=self.xs_sup[par].rearrange("p (d t) -> p d t", d=8),
                        in_=self.xs_d[s, :, :, u * DUO:(u + 1) * DUO].rearrange(
                            "d p t -> p d t")),
                        waits=[self.ev_xsduo_w[(s, u)], ev_xs_use[par], evA],
                        inc=16)
                ev_r = S.op("sy", lambda e, s=s, u=u, par=par: e.dma_start(
                    out=self.r_sup[par].rearrange("p (d t) -> p d t", d=8),
                    in_=self.r_d[s, :, :, u * DUO:(u + 1) * DUO].rearrange(
                        "d p t -> p d t")),
                    waits=[self.ev_rchunk_w[(s, 2 * u)], self.ev_rchunk_w[(s, 2 * u + 1)],
                           ev_r_use[par], evA], inc=16)
                ev_x = S.op("sy", lambda e, s=s, u=u: e.dma_start(
                    out=self.x_sup.rearrange("p (a t) -> p a t", a=8),
                    in_=self.xT_in[s, :, u * DUO:(u + 1) * DUO].rearrange(
                        "(a p) t -> p a t", p=128)),
                    waits=[ev_x_use, evA], inc=16)

                for h_t in range(NHT):
                    hp = h_t % 2
                    gb = self.pb[0 + 2 * hp]
                    ub = self.pb[1 + 2 * hp]
                    ev_gmm = ev_umm = None
                    for d_i in range(8):
                        ev_gmm = S.op("pe", lambda e, h_t=h_t, d_i=d_i, gb=gb, par=par: e.matmul(
                            gb[:, 0:DUO], self.wg_t(d_i, h_t * 128, 128),
                            self.xs_sup[par][:, d_i * DUO:(d_i + 1) * DUO],
                            start=(d_i == 0), stop=(d_i == 7)),
                            waits=[ev_xs, self.wgu_parts, gate_free[hp]] if d_i == 0 else (),
                            inc=1 if d_i == 7 else 0)
                    for d_i in range(8):
                        ev_umm = S.op("pe", lambda e, h_t=h_t, d_i=d_i, ub=ub, par=par: e.matmul(
                            ub[:, 0:DUO], self.wu_t(d_i, h_t * 128, 128),
                            self.xs_sup[par][:, d_i * DUO:(d_i + 1) * DUO],
                            start=(d_i == 0), stop=(d_i == 7)),
                            waits=[self.wgu_parts, up_free[hp]] if d_i == 0 else (),
                            inc=1 if d_i == 7 else 0)
                    ev_sl = S.op("ac", lambda e, gb=gb: e.activation(
                        self.silu_b[:], gb[:, 0:DUO], AF.Silu, scale=QS),
                        waits=[ev_gmm, ev_silu_use], inc=1)
                    gate_free[hp] = ev_sl
                    ev_hd = S.op("dv", lambda e, h_t=h_t, ub=ub: e.scalar_tensor_tensor(
                        self.hid_b[:, h_t * DUO:(h_t + 1) * DUO], ub[:, 0:DUO],
                        QS, self.silu_b[:], ALU.mult, ALU.mult),
                        waits=[ev_umm, ev_sl, ev_hid_use], inc=1)
                    up_free[hp] = ev_hd
                    ev_silu_use = ev_hd
                    if h_t == NHT // 2 - 1:
                        ev_hid_half = ev_hd
                    if idx == 0 and self._gn_tail is not None:
                        if h_t == 4:
                            self.gn_p2(self._gn_tail)
                        elif h_t == 6:
                            self.gn_p3(self._gn_tail)
                        elif h_t == 8:
                            self.gn_p4(self._gn_tail)
                ev_hid = S.ev("dv")
                ev_hid_use = None

                ev_down_last = None
                for d_i in range(8):
                    db = self.pb[4 + d_i % 4]
                    for h_t in range(NHT):
                        if h_t == 0:
                            w = [ev_hid_half, ev_wd, down_free[d_i % 4]]
                        elif h_t == NHT // 2:
                            w = [ev_hid]
                        else:
                            w = ()
                        S.op("pe", lambda e, d_i=d_i, h_t=h_t, db=db: e.matmul(
                            db[:, 0:DUO], self.wd_t(h_t, d_i * 128, 128),
                            self.hid_b[:, h_t * DUO:(h_t + 1) * DUO],
                            start=(h_t == 0), stop=(h_t == NHT - 1)),
                            waits=w,
                            inc=1 if h_t == NHT - 1 else 0)
                    ev_dmm = S.ev("pe")
                    ev_down_last = ev_dmm
                    ev_g = S.op("ac", lambda e, d_i=d_i, db=db: e.activation(
                        self.g_sup[:, d_i * DUO:(d_i + 1) * DUO], db[:, 0:DUO],
                        AF.Silu, scale=QS),
                        waits=[ev_dmm, ev_g_use]
                        + (first_g_extra if idx == 0 and d_i == 0 else []),
                        inc=1)
                    down_free[d_i % 4] = ev_g

                    op = d_i % 2
                    s_loc = self.s_tile
                    S.op("dv", lambda e, s=s, d_i=d_i, par=par: e.tensor_scalar(
                        s_loc[:], self.r_sup[par][:, d_i * DUO:(d_i + 1) * DUO],
                        self.acol[s][:, d_i:d_i + 1], self.bcol[s][:, d_i:d_i + 1],
                        ALU.mult, ALU.add),
                        waits=[ev_r, self.ev_acols[s]])
                    S.op("dv", lambda e, d_i=d_i: e.tensor_mul(
                        self.sg_tile[:], s_loc[:],
                        self.g_sup[:, d_i * DUO:(d_i + 1) * DUO]),
                        waits=[ev_g])
                    ev_ob = S.op("dv", lambda e, d_i=d_i, op=op: e.tensor_add(
                        self.out_b[op][:], self.x_sup[:, d_i * DUO:(d_i + 1) * DUO],
                        self.sg_tile[:]),
                        waits=[ev_x, ev_out_use[op]], inc=1)
                    ev_wr = S.op("gp",
                                 lambda e, s=s, u=u, d_i=d_i, op=op: e.dma_start(
                        out=self.out_d[s, d_i * 128:(d_i + 1) * 128,
                                       u * DUO:(u + 1) * DUO],
                        in_=self.out_b[op][:]),
                        waits=[ev_ob], inc=16)
                    ev_out_use[op] = ev_wr
                ev_g_use = S.ev("dv")
                ev_hid_use = ev_down_last
                ev_xs_use[par] = ev_down_last
                ev_r_use[par] = S.ev("dv")
                ev_x_use = S.ev("dv")

    def build(self):
        self.setup()
        self.phase_a()
        if self.with_b and self.stage == "full":
            self.phase_b()
        nc = self.nc
        S = self.S
        import contextlib
        with contextlib.ExitStack() as stack:
            sems = {}
            for name in ["pe", "dv", "ac", "po"] + [f"{q}{l}" for q in ("sy", "gp")
                                                    for l in range(NLANE)]:
                sems[name] = stack.enter_context(nc.semaphore(f"s_{name}"))
            S.set_sems(sems)
            with nc.Block() as block:
                @block.sync
                def _(e):
                    S.emit("sy", e)
                    for l in range(NLANE):
                        e.wait_ge(sems[f"sy{l}"], S.lane_n["sy"][l])

                @block.gpsimd
                def _(e):
                    S.emit("gp", e)
                    for l in range(NLANE):
                        e.wait_ge(sems[f"gp{l}"], S.lane_n["gp"][l])

                @block.vector
                def _(e):
                    S.emit("dv", e)

                @block.scalar
                def _(e):
                    S.emit("ac", e)

                @block.tensor
                def _(e):
                    S.emit("pe", e)
        return nc


def host_inputs(x_full, past_kv, w_qkv, gn_w, gn_b, w_gate, w_up, w_down):
    x_full = np.ascontiguousarray(np.asarray(x_full, np.float32))
    w_qkv = np.ascontiguousarray(np.asarray(w_qkv, np.float32))
    past_kv = np.asarray(past_kv, np.float32)
    gn_w = np.asarray(gn_w, np.float32)
    gn_b = np.asarray(gn_b, np.float32)
    wgT = np.ascontiguousarray(np.asarray(w_gate, np.int32).T)
    wuT = np.ascontiguousarray(np.asarray(w_up, np.int32).T)
    wdT = np.ascontiguousarray(np.asarray(w_down, np.int32).T)

    gam = 1.0 - 2.0 ** (-5.0 - np.arange(NH, dtype=np.float64))
    i_loc = np.arange(CH, dtype=np.float64)
    gqf = np.zeros((128, NPAIR * CH), np.float64)
    gkf = np.zeros((128, NPAIR * CH), np.float64)
    for p in range(NPAIR):
        for half, h in ((0, 2 * p), (1, 2 * p + 1)):
            rows = slice(half * 64, half * 64 + 64)
            gqf[rows, p * CH:(p + 1) * CH] = gam[h] ** i_loc
            gkf[rows, p * CH:(p + 1) * CH] = gam[h] ** (-i_loc)
    g0q = np.zeros((128, NCH * NPAIR), np.float64)
    g0k = np.zeros((128, NCH * NPAIR), np.float64)
    for c in range(NCH):
        for p in range(NPAIR):
            for half, h in ((0, 2 * p), (1, 2 * p + 1)):
                rows = slice(half * 64, half * 64 + 64)
                g0q[rows, c * NPAIR + p] = gam[h] ** (128.0 * c)
                g0k[rows, c * NPAIR + p] = gam[h] ** (-128.0 * c)
    gkt = np.zeros((NCH, 128, D), np.float64)
    j = np.arange(CH, dtype=np.float64)
    for c in range(NCH):
        for h in range(NH):
            gkt[c, :, h * HD:(h + 1) * HD] = (gam[h] ** (-(128.0 * c + j)))[:, None]
    mask = np.zeros((128, 2 * CH), np.float32)
    tri = (np.arange(CH)[None, :] >= np.arange(CH)[:, None]).astype(np.float32)
    mask[:, 0:CH] = tri
    mask[:, CH:2 * CH] = tri

    pmat = np.zeros((128, NPAIR * HD), np.float32)
    for p in range(NPAIR):
        pmat[0:64, p * HD:(p + 1) * HD] = past_kv[2 * p]
        pmat[64:128, p * HD:(p + 1) * HD] = past_kv[2 * p + 1]
    gnw = np.zeros((128, NPAIR), np.float32)
    gnb = np.zeros((128, NPAIR), np.float32)
    for p in range(NPAIR):
        gnw[0:64, p] = gn_w[2 * p]
        gnw[64:128, p] = gn_w[2 * p + 1]
        gnb[0:64, p] = gn_b[2 * p]
        gnb[64:128, p] = gn_b[2 * p + 1]

    shared = dict(
        wqkv=w_qkv, wgT=wgT, wuT=wuT, wdT=wdT,
        pmat=pmat, gnw=gnw, gnb=gnb,
        gqf=gqf.astype(BFNP), gkf=gkf.astype(BFNP),
        g0q=g0q.astype(np.float32), g0k=g0k.astype(np.float32),
        gkt=gkt.astype(BFNP), mask=mask.astype(BFNP),
        ident=np.eye(128, dtype=np.float32),
    )
    in_maps = []
    for core in range(NCORE):
        xs = np.ascontiguousarray(x_full[core * BS:(core + 1) * BS])
        xT = np.ascontiguousarray(xs.transpose(0, 2, 1))
        m = dict(shared)
        m["x"] = xs
        m["xT"] = xT
        in_maps.append(m)
    return in_maps


def kernel(L0_attn_x, L0_attn_past_kv, L0_attn_W_qkv, L0_attn_gn_weight,
           L0_attn_gn_bias, L0_ffn_w_gate, L0_ffn_w_up, L0_ffn_w_down):
    in_maps = host_inputs(L0_attn_x, L0_attn_past_kv, L0_attn_W_qkv,
                          L0_attn_gn_weight, L0_attn_gn_bias,
                          L0_ffn_w_gate, L0_ffn_w_up, L0_ffn_w_down)
    nc = Builder().build()
    res = run_bass_kernel_spmd(nc, in_maps, core_ids=list(range(NCORE)))
    parts = [np.asarray(r["outT"], np.float32).transpose(0, 2, 1)
             for r in res.results]
    return np.ascontiguousarray(np.concatenate(parts, axis=0))


# version 34 (final)
# speedup vs baseline: 1.0185x; 1.0185x over previous
import numpy as np
import ml_dtypes
import concourse.bass as bass
import concourse.mybir as mybir
from concourse.bass_utils import run_bass_kernel_spmd

BF16 = mybir.dt.bfloat16
F32 = mybir.dt.float32
I32 = mybir.dt.int32
AF = mybir.ActivationFunctionType
ALU = mybir.AluOpType
AX = mybir.AxisListType

B, T, D, NH, HD = 16, 2048, 1024, 16, 64
NCORE = 8
BS = B // NCORE
CH = 128
NCH = T // CH
DUO = 256
NDUO = T // DUO
HIDDEN = 3072
NHT = HIDDEN // 128
EPS = 1e-5
NPAIR = 8
QS = 1.0 / 127.0
GN_N = float(D * T)

BFNP = ml_dtypes.bfloat16


def _flat(waits):
    out = []
    for w in waits:
        if w is None:
            continue
        if isinstance(w, (list, tuple)) and w and isinstance(w[0], (list, tuple, type(None))):
            out.extend(_flat(w))
        elif isinstance(w, list):
            out.extend(_flat(w))
        else:
            out.append(w)
    return out


NLANE = 4
STAGES = ["proj", "kv", "scores", "inner", "nocross", "nozd", "full"]


class Sched:

    def __init__(self, nc, names, queues, auto_chain=("dv", "ac")):
        self.nc = nc
        self.e = {}
        self.queues = set(queues)
        self.auto_chain = set(auto_chain)
        self.sems = {}
        for name in list(names) + list(queues):
            self.e[name] = dict(n=0, ops=[], seen={})
        self.lane_n = {q: [0] * NLANE for q in queues}
        self.lane_i = {q: 0 for q in queues}
        self.pinned = {"gp": (3,)}
        self.po_n = 0

    def set_sems(self, sems):
        self.sems = sems

    def _add(self, eng, fn, waits, incs, force_waits=()):
        rec = self.e[eng]
        ws = [tuple(ev) for ev in _flat(force_waits)]
        for ev in _flat(waits):
            ename, val = ev
            if rec["seen"].get(ename, -1) >= val:
                continue
            rec["seen"][ename] = val
            ws.append((ename, val))
        rec["ops"].append((fn, ws, incs))

    def op(self, eng, fn, waits=(), inc=0, force_waits=(), lane=None):
        if eng in self.queues:
            assert inc == 16
            if lane is None:
                lane = self.lane_i[eng]
                self.lane_i[eng] = (lane + 1) % NLANE
                if self.lane_i[eng] in self.pinned.get(eng, ()):
                    self.lane_i[eng] = (self.lane_i[eng] + 1) % NLANE
            lname = f"{eng}{lane}"
            prev = self.lane_n[eng][lane]
            ws = list(waits) + [(lname, prev)]
            self.lane_n[eng][lane] = prev + 16
            self._add(eng, fn, ws, [(lname, 16)], force_waits=force_waits)
            return (lname, prev + 16)
        if eng in self.auto_chain:
            waits = list(waits) + [(eng, self.e[eng]["n"])]
            inc = 1
        elif inc:
            waits = list(waits) + [(eng, self.e[eng]["n"])]
        self.e[eng]["n"] += inc
        self._add(eng, fn, waits, [(eng, inc)] if inc else [],
                  force_waits=force_waits)
        return (eng, self.e[eng]["n"]) if inc else None

    def cop(self, fn, waits=(), force_waits=()):
        ws = list(waits) + [("po", self.po_n)]
        self.po_n += 1
        self._add("gp", fn, ws, [("po", 1)], force_waits=force_waits)
        return ("po", self.po_n)

    def ev(self, eng):
        if eng == "po":
            return ("po", self.po_n)
        if eng in self.queues:
            return [(f"{eng}{l}", self.lane_n[eng][l]) for l in range(NLANE)]
        return (eng, self.e[eng]["n"])

    def emit(self, eng_name, engine):
        for fn, ws, incs in self.e[eng_name]["ops"]:
            for ename, val in ws:
                engine.wait_ge(self.sems[ename], val)
            inst = fn(engine)
            for iname, ival in incs:
                inst.then_inc(self.sems[iname], ival)


class Builder:
    def __init__(self, expose=(), n_samples=BS, n_duos=NDUO, with_b=True,
                 with_ret=True, stage="full", proj_evac="full"):
        self.proj_evac = proj_evac
        self.n_samples = n_samples
        self.n_duos = n_duos
        self.with_b = with_b
        self.with_ret = with_ret
        self.stage = stage
        self.expose = expose
        nc = bass.Bass("TRN2", target_bir_lowering=False)
        self.nc = nc
        self.S = Sched(nc, ["pe", "dv", "ac"], queues=["sy", "gp"])

        di = lambda name, shape, dtype: nc.dram_tensor(
            name, shape, dtype, kind="ExternalInput").ap()

        def dsc(name, shape, dtype):
            kind = "ExternalOutput" if name in expose else "Internal"
            return nc.dram_tensor(name, shape, dtype, kind=kind).ap()

        self.x_in = di("x", [BS, T, D], F32)
        self.xT_in = di("xT", [BS, D, T], F32)
        self.wqkv_in = di("wqkv", [D, 3 * D], F32)
        self.wgT_in = di("wgT", [D, HIDDEN], I32)
        self.wuT_in = di("wuT", [D, HIDDEN], I32)
        self.wdT_in = di("wdT", [HIDDEN, D], I32)
        self.p_in = di("pmat", [128, NPAIR * HD], F32)
        self.gnw_in = di("gnw", [128, NPAIR], F32)
        self.gnb_in = di("gnb", [128, NPAIR], F32)
        self.gqf_in = di("gqf", [128, NPAIR * CH], BF16)
        self.gkf_in = di("gkf", [128, NPAIR * CH], BF16)
        self.g0q_in = di("g0q", [128, NCH * NPAIR], F32)
        self.g0k_in = di("g0k", [128, NCH * NPAIR], F32)
        self.gkt_in = di("gkt", [NCH, 128, D], BF16)
        self.mask_in = di("mask", [128, 2 * CH], BF16)
        self.ident_in = di("ident", [128, 128], F32)

        self.out_d = nc.dram_tensor("outT", [BS, D, T], F32,
                                    kind="ExternalOutput").ap()
        self.xs_d = dsc("xs_sc", [BS, 8, 128, T], BF16)
        self.r_d = dsc("r_sc", [BS, 8, 128, T], BF16)

        sb = nc.alloc_sbuf_tensor
        self.wg_a = sb("wg_a", [128, 8 * HIDDEN], BF16)
        self.wu_a = sb("wu_a", [128, 8 * HIDDEN], BF16)
        self.wd_a = sb("wd_a", [128, 8 * HIDDEN], BF16)

        self.ident = sb("identS", [128, 128], F32)
        self.mask2 = sb("mask2", [128, 2 * CH], BF16)
        self.p_bf = sb("p_bf", [128, NPAIR * HD], BF16)
        self.gnw_c = sb("gnw_c", [128, NPAIR], F32)
        self.gnb_c = sb("gnb_c", [128, NPAIR], F32)
        self.gqf = sb("gqfS", [128, NPAIR * CH], BF16)
        self.gkf = sb("gkfS", [128, NPAIR * CH], BF16)
        self.g0q = sb("g0qS", [128, NCH * NPAIR], F32)
        self.g0k = sb("g0kS", [128, NCH * NPAIR], F32)
        self.ones_row = sb("ones_row", [1, 128], BF16)
        self.ones_col = sb("ones_col", [128, 1], BF16)
        self.eps_col = sb("eps_col", [128, 1], F32)
        self.z_f32 = sb("z_f32a", [128, 512], F32)
        self.z_bf = sb("z_bfS", [128, 512], BF16)
        self.acol = [sb("acol0", [128, NPAIR], F32), sb("acol1", [128, NPAIR], F32)]
        self.bcol = [sb("bcol0", [128, NPAIR], F32), sb("bcol1", [128, NPAIR], F32)]

        qkq = sb("qkq", [128, 3 * NPAIR * DUO], BF16)
        self.qp_b = qkq[:, 0:2048]
        self.kp_b = qkq[:, 2048:4096]
        self.qr_b = qkq[:, 4096:6144]
        self.hid_b = qkq[:]

        xsd0 = sb("xsd0", [128, 8 * DUO], BF16)
        xsd1 = sb("xsd1", [128, 8 * DUO], BF16)
        self.xs_duo = [xsd0[:], xsd1[:]]
        self.g_sup = xsd0[:]
        gkt2 = sb("gkt2", [128, 2 * D], BF16)
        self.gkt_b = [gkt2[:, 0:D], gkt2[:, D:2 * D]]
        self.xs_sup = [xsd1[:], gkt2[:]]
        xst = sb("xstok", [128, 2 * D], BF16)
        self.xs_tok = [xst[:, 0:D], xst[:, D:2 * D]]
        ktv = sb("ktv", [128, 2 * D], BF16)
        ktvb = sb("ktvb", [128, 2 * D], BF16)
        self.ktok = [ktv[:, 0:D], ktvb[:, 0:D]]
        self.vtok = [ktv[:, D:2 * D], ktvb[:, D:2 * D]]
        self.r_sup = [xst[:], ktv[:]]

        f32a = sb("f32a", [128, 2816], F32)
        self.x_tok = [f32a[:, 0:D], f32a[:, 1792:1792 + D]]
        self.p_f32 = f32a[:, D:D + 512]
        self.rsum_c = f32a[:, 1536:1536 + NCH * NPAIR]
        self.rsq_c = f32a[:, 1664:1664 + NCH * NPAIR]
        self.x_sup = f32a[:, 0:2048]
        self.out_b = [f32a[:, 2048:2048 + DUO], f32a[:, 2304:2304 + DUO]]

        self.identb = sb("identB", [128, 128], BF16)
        self.at_p = [sb("at_p0", [128, 2 * CH], BF16), sb("at_p1", [128, 2 * CH], BF16)]
        self.r_pair = [sb("r_pair0", [128, CH], BF16), sb("r_pair1", [128, CH], BF16)]
        self.sq_scr = sb("sq_scr", [128, D], BF16)
        self.sq_scr2 = sb("sq_scr2", [128, CH], BF16)
        self.stat_c = [sb("stat_c0", [128, 8], F32), sb("stat_c1", [128, 8], F32)]
        self.rows2 = sb("rows2", [1, 2 * CH], BF16)
        self.stats2 = sb("stats2", [128, 2], F32)
        self.stats2b = sb("stats2b", [128, 2], BF16)
        self.tot_sb = sb("tot_sb", [2, 1], F32)
        self.trow = sb("trow", [1, 2], F32)
        self.sc_sb = sb("sc_sb", [1, 8], F32)
        self.vals2 = sb("vals2", [1, 2], BF16)
        self.bc2 = sb("bc2", [128, 2], F32)

        self.silu_b = self.sq_scr[:, 0:DUO]
        self.s_tile = self.sq_scr[:, DUO:2 * DUO]
        self.sg_tile = self.sq_scr[:, 2 * DUO:3 * DUO]

        self.pb = [nc.alloc_psum_tensor(f"pb{i}", [128, 512], F32) for i in range(8)]
        self.tp_ps = self.pb[7][0:1, 256:384]
        self.tot_ps = self.pb[7][0:2, 384:385]
        self.tr2_ps = self.pb[7][0:1, 386:388]
        self.bc2_ps = self.pb[7][:, 388:390]

        self._gn_tail = None
        self.bank_free = [None] * 8
        self.ev_ktv_use = [None, None]
        self.ev_xsduo_dma = [None, None]
        self.ev_zbf = None
        self.ev_zuse = None
        self.ev_xsduo_w = {}
        self.ev_rchunk_w = {}
        self.ev_acols = [None, None]
        self.ev_wqkv_lastuse = None
        self.ev_gktuse = [None, None]
        self.ev_gkt_ev = [None, None]
        self.ev_xstok_use = [None, None]
        self.ev_xtok_use = [None, None]
        self.ev_xt_dma = [None, None]
        self.ev_statc_use = [None, None]
        self.ev_qkbuf_use = None
        self.ev_xsduo_use = [None, None]
        self.ev_at_use = [None, None]
        self.ev_rpair_use = [None, None]
        self.ev_tmp_use = [None, None]
        self.ev_gn_done = None

    def wqkv_t(self, d_i, c0, w):
        return self.wd_a[:, d_i * 3 * D + c0: d_i * 3 * D + c0 + w]

    def wg_t(self, d_i, h0, w):
        return self.wg_a[:, d_i * HIDDEN + h0: d_i * HIDDEN + h0 + w]

    def wu_t(self, d_i, h0, w):
        return self.wu_a[:, d_i * HIDDEN + h0: d_i * HIDDEN + h0 + w]

    def wd_t(self, h_t, c0, w):
        return self.wd_a[:, h_t * D + c0: h_t * D + c0 + w]

    def setup(self):
        S = self.S
        self.ev_const = {}
        self.ev_const["ident"] = S.op(
            "sy", lambda e: e.dma_start(out=self.ident[:], in_=self.ident_in),
            inc=16)
        self.ln_stats_dma(0, 0)
        self.ln_stats_dma(0, 1)
        for key, dst, src in (
            ("mask", self.mask2, self.mask_in),
            ("pmat", self.p_f32, self.p_in), ("gnw", self.gnw_c, self.gnw_in),
            ("gnb", self.gnb_c, self.gnb_in), ("gqf", self.gqf, self.gqf_in),
            ("gkf", self.gkf, self.gkf_in), ("g0q", self.g0q, self.g0q_in),
            ("g0k", self.g0k, self.g0k_in),
        ):
            self.ev_const[key] = S.op(
                "sy", lambda e, d=dst, s=src: e.dma_start(out=d[:], in_=s), inc=16)

        self.ev_wqkv = S.op("gp", lambda e: e.dma_start(
            out=self.wd_a[:].rearrange("p (a c) -> p a c", a=8),
            in_=self.wqkv_in.rearrange("(a p) c -> p a c", p=128)), inc=16)

        self.ev_identb = S.op("dv", lambda e: e.tensor_copy(
            self.identb[:], self.ident[:]), waits=[self.ev_const["ident"]], inc=1)
        NWARM = 430
        for i in range(NWARM):
            S.op("pe", lambda e, b=i % 2: e.matmul(
                self.pb[b][:, 0:128], self.identb[:], self.identb[:],
                start=True, stop=True),
                waits=[self.ev_identb] if i == 0 else (),
                inc=1 if i == NWARM - 1 else 0)
        self.bank_free[0] = S.ev("pe")
        self.bank_free[1] = S.ev("pe")
        self.ev_ones = S.op("dv", lambda e: e.memset(self.ones_row[:], 1.0), inc=1)
        self.ev_onesc = S.op("dv", lambda e: e.memset(self.ones_col[:], 1.0), inc=1)
        self.ev_eps = S.op("dv", lambda e: e.memset(self.eps_col[:], EPS), inc=1)
        self.ev_z0 = S.op("dv", lambda e: e.memset(self.z_f32[:], 0.0), inc=1)
        self.ev_pbf = S.op("dv", lambda e: e.tensor_scalar_mul(
            self.p_bf[:], self.p_f32[:], float(NH * HD)),
            waits=[self.ev_const["pmat"]], inc=1)

    def ln_stats_dma(self, s, c):
        par = c % 2
        self.ev_xt_dma[par] = self.S.op("sy", lambda e, par=par: e.dma_start(
            out=self.x_tok[par][:], in_=self.x_in[s, c * CH:(c + 1) * CH, :]),
            waits=[self.ev_xtok_use[par]], inc=16)

    def ln_stats_pre_items(self, s, c, box, key):
        S = self.S
        par = c % 2
        st = self.stat_c[par]
        ev_xt = self.ev_xt_dma[par]
        use0 = self.ev_statc_use[par]

        def _rs():
            ev = S.op("dv", lambda e, par=par: e.reduce_sum(
                st[:, 0:1], self.x_tok[par][:], axis=AX.X),
                waits=[ev_xt, use0], inc=1)
            self.ev_xtok_use[par] = [ev]
            box[(key, 'rs')] = ev

        def _sq():
            ev = S.op("ac", lambda e, par=par: e.activation(
                self.sq_scr[:], self.x_tok[par][:], AF.Square,
                accum_out=st[:, 1:2]),
                waits=[ev_xt, use0], inc=1)
            self.ev_xtok_use[par].append(ev)

        def _m2():
            S.op("ac", lambda e: e.mul(st[:, 2:3], st[:, 0:1], 1.0 / D),
                 waits=[box[(key, 'rs')]])

        def _m3():
            S.op("ac", lambda e: e.mul(st[:, 3:4], st[:, 1:2], -1.0 / D))

        def _m4():
            S.op("ac", lambda e: e.mul(st[:, 4:5], st[:, 2:3], st[:, 2:3]))

        def _a4():
            S.op("ac", lambda e: e.add(st[:, 4:5], st[:, 4:5], st[:, 3:4]))

        def _sqrt():
            S.op("ac", lambda e: e.activation(st[:, 5:6], st[:, 4:5], AF.Sqrt,
                                              scale=-1.0, bias=self.eps_col[:]),
                 waits=[self.ev_eps], inc=1)
            box[key] = S.ev("ac")

        return [_rs, _sq, _m2, _m3, _m4, _a4, _sqrt]

    def ln_stats_pre(self, s, c):
        box = {}
        for f in self.ln_stats_pre_items(s, c, box, 'k'):
            f()
        return box['k']

    def ln_stats_fin(self, s, c, ev_std):
        S = self.S
        par = c % 2
        st = self.stat_c[par]
        ev_rec = S.op("dv", lambda e: e.reciprocal(st[:, 6:7], st[:, 5:6]),
                      waits=[ev_std], inc=1)
        S.op("dv", lambda e: e.scalar_tensor_tensor(
            st[:, 7:8], st[:, 2:3], -1.0, st[:, 6:7], ALU.mult, ALU.mult),
            force_waits=[ev_rec], inc=1)
        ev_xst = S.op("dv", lambda e, par=par: e.tensor_scalar(
            self.xs_tok[par][:], self.x_tok[par][:], st[:, 6:7], st[:, 7:8],
            ALU.mult, ALU.add),
            waits=[self.ev_xstok_use[par]], inc=1)
        self.ev_statc_use[par] = ev_xst
        self.ev_xtok_use[par] = self.ev_xtok_use[par] + [ev_xst]
        return ev_xst

    def xsT_group(self, s, c, half, bank, ev_xst):
        S = self.S
        duo, par = c // 2, c % 2
        dbuf = self.xs_duo[duo % 2]
        evs_t = []
        for i in range(4):
            d_i = half * 4 + i
            evs_t.append(S.op("pe", lambda e, i=i, d_i=d_i, bank=bank: e.transpose(
                self.pb[bank][:, i * 64:(i + 1) * 64].bitcast(BF16),
                self.xs_tok[par][:, d_i * 128:(d_i + 1) * 128],
                self.identb[:]),
                waits=[ev_xst, self.ev_identb, self.bank_free[bank]]
                if i == 0 else (), inc=1))
        first_w = [self.ev_xsduo_use[duo % 2], self.ev_xsduo_dma[duo % 2]] \
            if (par == 0 and half == 0) else []
        for i in range(4):
            d_i = half * 4 + i
            S.op("ac", lambda e, i=i, d_i=d_i, bank=bank, par=par: e.copy(
                dbuf[:, d_i * DUO + par * CH: d_i * DUO + (par + 1) * CH],
                self.pb[bank][:, i * 64:(i + 1) * 64].bitcast(BF16)),
                waits=[evs_t[i]] + first_w, inc=1)
        ev = S.ev("ac")
        self.bank_free[bank] = ev
        self.ev_xstok_use[par] = evs_t[-1]
        return ev

    def qk_proj(self, s, duo, ev_xs, drip=None):
        S = self.S
        drip = list(drip) if drip else []
        dbuf = self.xs_duo[duo % 2]
        for rnd in range(8):
            if rnd > 0:
                for _ in range(2):
                    if drip:
                        drip.pop(0)()
            bnk = (4, 5, 0, 1)[rnd % 4]
            is_q = rnd < 4
            p0 = (rnd % 4) * 2
            col0 = 0 if is_q else D
            for pi in range(2):
                p = p0 + pi
                for d_i in range(8):
                    S.op("pe", lambda e, p=p, pi=pi, d_i=d_i, bnk=bnk, col0=col0: e.matmul(
                        self.pb[bnk][:, pi * DUO:(pi + 1) * DUO],
                        self.wqkv_t(d_i, col0 + p * 128, 128),
                        dbuf[:, d_i * DUO:(d_i + 1) * DUO],
                        start=(d_i == 0), stop=(d_i == 7)),
                        waits=[ev_xs, self.ev_wqkv, self.bank_free[bnk]]
                        if (d_i == 0 and pi == 0) else (),
                        inc=1 if (d_i == 7 and pi == 1) else 0)
            ev_mm = S.ev("pe")
            self.ev_wqkv_lastuse = ev_mm
            if self.proj_evac == "none":
                ev_dv = S.op("dv", lambda e, bnk=bnk, is_q=is_q: e.tensor_copy(
                    (self.qp_b if is_q else self.kp_b)[:, 0:512],
                    self.pb[bnk][:]),
                    waits=[ev_mm, self.ev_qkbuf_use], inc=1)
                self.bank_free[bnk] = ev_dv
                continue
            n_dv = 0
            for pi in range(2):
                p = p0 + pi
                for cc in range(2):
                    cg = duo * 2 + cc
                    src = self.pb[bnk][:, pi * DUO + cc * CH: pi * DUO + (cc + 1) * CH]
                    n_dv += 1
                    last_dv = n_dv == 4
                    if self.proj_evac in ("copy", "copyac"):
                        S.op("dv", lambda e, p=p, cc=cc, src=src, is_q=is_q: e.tensor_copy(
                            (self.qp_b if is_q else self.kp_b)[
                                :, p * DUO + cc * CH: p * DUO + (cc + 1) * CH], src),
                            waits=[ev_mm, self.ev_qkbuf_use], inc=1 if last_dv else 0)
                    elif is_q:
                        S.op("dv", lambda e, p=p, cc=cc, cg=cg, src=src: e.scalar_tensor_tensor(
                            self.qp_b[:, p * DUO + cc * CH: p * DUO + (cc + 1) * CH],
                            src, self.g0q[:, cg * NPAIR + p: cg * NPAIR + p + 1],
                            self.gqf[:, p * CH:(p + 1) * CH], ALU.mult, ALU.mult),
                            waits=[ev_mm, self.ev_qkbuf_use, self.ev_const["g0q"],
                                   self.ev_const["gqf"]],
                            inc=1 if last_dv else 0)
                    else:
                        S.op("dv", lambda e, p=p, cc=cc, cg=cg, src=src: e.scalar_tensor_tensor(
                            self.kp_b[:, p * DUO + cc * CH: p * DUO + (cc + 1) * CH],
                            src, self.g0k[:, cg * NPAIR + p: cg * NPAIR + p + 1],
                            self.gkf[:, p * CH:(p + 1) * CH], ALU.mult, ALU.mult),
                            waits=[ev_mm, self.ev_qkbuf_use, self.ev_const["g0k"],
                                   self.ev_const["gkf"]],
                            inc=1 if last_dv else 0)
            ev_dv = S.ev("dv")
            ev_ac = None
            if is_q and self.proj_evac in ("full", "copyac"):
                for pi in range(2):
                    p = p0 + pi
                    S.op("ac", lambda e, p=p, pi=pi, bnk=bnk: e.copy(
                        self.qr_b[:, p * DUO:(p + 1) * DUO],
                        self.pb[bnk][:, pi * DUO:(pi + 1) * DUO]),
                        waits=[ev_mm, self.ev_qkbuf_use, ev_dv],
                        inc=1 if pi == 1 else 0)
                ev_ac = S.ev("ac")
            self.bank_free[bnk] = [ev_dv, ev_ac]
        ev_ret = [S.ev("dv"), S.ev("ac")]
        for f in drip:
            f()
        return ev_ret

    def ktv_gkt_load(self, c):
        par = c % 2
        self.ev_gkt_ev[par] = self.S.op("sy", lambda e: e.dma_start(
            out=self.gkt_b[par][:], in_=self.gkt_in[c]),
            waits=[self.ev_gktuse[par]], inc=16)

    def ktv_group(self, s, c, grp, bank, ev_xs):
        S = self.S
        duo, par = c // 2, c % 2
        dbuf = self.xs_duo[duo % 2]
        is_k = grp < 2
        half = grp % 2
        col0 = (D if is_k else 2 * D) + half * 512
        for d_i in range(8):
            S.op("pe", lambda e, d_i=d_i, bank=bank, col0=col0: e.matmul(
                self.pb[bank][:, 0:512],
                dbuf[:, d_i * DUO + par * CH: d_i * DUO + (par + 1) * CH],
                self.wqkv_t(d_i, col0, 512),
                start=(d_i == 0), stop=(d_i == 7)),
                waits=[ev_xs, self.ev_wqkv, self.bank_free[bank]] if d_i == 0 else (),
                inc=1 if d_i == 7 else 0)
        ev_mm = S.ev("pe")
        cols = slice(half * 512, (half + 1) * 512)
        if is_k:
            ev = S.op("dv", lambda e, bank=bank, cols=cols, par=par: e.tensor_tensor(
                self.ktok[par][:, cols], self.pb[bank][:, 0:512],
                self.gkt_b[par][:, cols], ALU.mult),
                waits=[ev_mm, self.ev_gkt_ev[par], self.ev_ktv_use[par]], inc=1)
            self.ev_gktuse[par] = ev
        else:
            ev = S.op("ac", lambda e, bank=bank, cols=cols, par=par: e.copy(
                self.vtok[par][:, cols], self.pb[bank][:, 0:512]),
                waits=[ev_mm, self.ev_ktv_use[par]], inc=1)
        self.bank_free[bank] = ev
        return ev

    def retention(self, s, c, ev_kev, ev_vev, ev_qk, mid_cbs=None,
                  skip_zd=False):
        S = self.S
        lvl = STAGES.index(self.stage)
        do_inner = lvl >= 3
        do_merge = lvl >= 4
        do_cross = lvl >= 5
        do_zd = lvl >= 6 and not skip_zd
        duo, par = c // 2, c % 2
        ccol = par * CH
        ktok, vtok = self.ktok[par], self.vtok[par]
        mid_cbs = mid_cbs or {}

        ev_mask = [None] * NPAIR

        def front(p):
            pp = p % 2
            sb_bank = self.pb[pp]
            qcols = slice(p * DUO + ccol, p * DUO + ccol + CH)
            S.op("pe", lambda e, sb_bank=sb_bank, qcols=qcols: e.matmul(
                sb_bank[:, 0:CH], self.kp_b[0:64, qcols], self.qp_b[0:64, qcols],
                start=True, stop=True),
                waits=[ev_qk, self.bank_free[pp]])
            ev_sc = S.op("pe", lambda e, sb_bank=sb_bank, qcols=qcols: e.matmul(
                sb_bank[:, CH:2 * CH], self.kp_b[64:128, qcols],
                self.qp_b[64:128, qcols], start=True, stop=True), inc=1)

            ev_mk = S.op("dv", lambda e, pp=pp, sb_bank=sb_bank: e.tensor_tensor(
                self.at_p[pp][:], sb_bank[:, 0:2 * CH], self.mask2[:], ALU.mult),
                waits=[ev_sc, self.ev_at_use[pp], self.ev_const["mask"]], inc=1)
            ev_mask[p] = ev_mk
            self.bank_free[pp] = ev_mk

        def back(p):
            pp = p % 2
            ci_bank = self.pb[2 + pp]
            qcols = slice(p * DUO + ccol, p * DUO + ccol + CH)
            if not do_inner and not do_cross:
                return
            first_waits = [self.bank_free[2 + pp], ev_mask[p], ev_vev]
            if do_cross:
                first_waits += [self.ev_pbf, ev_qk]
                if c > 0:
                    first_waits.append(self.ev_zbf)
            first = True
            for base in (0, 64):
                rows = slice(base, base + 64)
                vcols = slice(p * 128 + base, p * 128 + base + 64)
                acols = slice(0, CH) if base == 0 else slice(CH, 2 * CH)
                if do_cross:
                    if c > 0:
                        S.op("pe", lambda e, p=p, rows=rows, ci_bank=ci_bank, qcols=qcols: e.matmul(
                            ci_bank[rows, CH:2 * CH], self.z_bf[rows, p * 64:(p + 1) * 64],
                            self.qp_b[rows, qcols], start=True, stop=False,
                            skip_group_check=True),
                            waits=first_waits if first else (),
                            inc=1 if (base == 64 and p == NPAIR - 1) else 0)
                        first = False
                        S.op("pe", lambda e, p=p, rows=rows, ci_bank=ci_bank, qcols=qcols: e.matmul(
                            ci_bank[rows, CH:2 * CH], self.p_bf[rows, p * 64:(p + 1) * 64],
                            self.qr_b[rows, qcols], start=False, stop=False,
                            skip_group_check=True))
                    else:
                        S.op("pe", lambda e, p=p, rows=rows, ci_bank=ci_bank, qcols=qcols: e.matmul(
                            ci_bank[rows, CH:2 * CH], self.p_bf[rows, p * 64:(p + 1) * 64],
                            self.qr_b[rows, qcols], start=True, stop=False,
                            skip_group_check=True),
                            waits=first_waits if first else ())
                        first = False
                if do_inner:
                    S.op("pe", lambda e, rows=rows, vcols=vcols, acols=acols,
                         ci_bank=ci_bank, st=not do_cross: e.matmul(
                        ci_bank[rows, CH:2 * CH], vtok[:, vcols],
                        self.at_p[pp][:, acols], start=st, stop=True,
                        skip_group_check=True),
                        waits=first_waits if first else (),
                        inc=1 if base == 64 else 0)
                    first = False
            if do_cross and c > 0 and p == NPAIR - 1:
                self.ev_zuse = S.ev("pe")
            if not do_inner:
                return
            ev_inner = S.ev("pe")
            self.ev_at_use[pp] = ev_inner

            if not do_merge:
                return
            col = c * NPAIR + p
            ev_mg = S.op("dv", lambda e, pp=pp, ci_bank=ci_bank, col=col: e.tensor_scalar(
                self.r_pair[pp][:], ci_bank[:, CH:2 * CH], 1.0, 0.0,
                ALU.mult, ALU.add, accum_out=self.rsum_c[:, col:col + 1]),
                waits=[ev_inner, self.ev_rpair_use[pp], self.ev_gn_done], inc=1)
            self.bank_free[2 + pp] = ev_mg
            ev_sq = S.op("ac", lambda e, pp=pp, col=col: e.activation(
                self.sq_scr2[:], self.r_pair[pp][:], AF.Square,
                accum_out=self.rsq_c[:, col:col + 1]),
                waits=[ev_mg, self.ev_gn_done], inc=1)
            ev_rw = S.op("gp", lambda e, s=s, p=p, c=c, pp=pp: e.dma_start(
                out=self.r_d[s, p, :, c * CH:(c + 1) * CH], in_=self.r_pair[pp][:]),
                waits=[ev_mg], inc=16)
            self.ev_rpair_use[pp] = [ev_sq, ev_rw]
            if p == NPAIR - 1:
                self.ev_rchunk_w[(s, c)] = ev_rw

        def zdelta():
            first = True
            for p in range(NPAIR):
                for base in (0, 64):
                    rows = slice(base, base + 64)
                    koff = p * 128 + base
                    S.op("pe", lambda e, p=p, rows=rows, koff=koff: e.matmul(
                        self.pb[6][rows, p * 64:(p + 1) * 64],
                        ktok[:, koff:koff + 64], vtok[:, koff:koff + 64],
                        start=True, stop=True),
                        waits=[ev_kev, ev_vev, self.bank_free[6]] if first else (),
                        inc=1 if (p == NPAIR - 1 and base == 64) else 0)
                    first = False

        front(0)
        front(1)
        if do_zd:
            zdelta()
        for p in range(2, NPAIR):
            back(p - 2)
            if p in mid_cbs:
                mid_cbs[p]()
            front(p)
        if 'tail' in mid_cbs:
            mid_cbs['tail']()
        back(NPAIR - 2)
        back(NPAIR - 1)

        if do_zd:
            ev_zd = S.ev("pe")
            ev_zadd = S.op("dv", lambda e: e.tensor_tensor(
                self.z_f32[:], self.z_f32[:], self.pb[6][:], ALU.add),
                waits=[ev_zd, self.ev_z0], inc=1)
            self.bank_free[6] = ev_zadd
            self.ev_zbf = S.op("dv", lambda e: e.tensor_copy(
                self.z_bf[:], self.z_f32[:]),
                waits=[self.ev_zuse], inc=1)

        self.ev_qkbuf_use = S.ev("pe")

    def gn_finalize(self, s):
        self.gn_p1(s)
        self.gn_p2(s)
        self.gn_p3(s)
        self.gn_p4(s)

    def gn_p1(self, s):
        S = self.S
        ev_sums = [S.ev("dv"), S.ev("ac")]
        S.op("ac", lambda e: e.activation(
            self.p_f32[:, 0:128], self.rsum_c[:], AF.Copy,
            accum_out=self.stats2[:, 0:1]),
            waits=ev_sums)
        S.op("ac", lambda e: e.activation(
            self.p_f32[:, 128:256], self.rsq_c[:], AF.Copy,
            accum_out=self.stats2[:, 1:2]))
        S.op("ac", lambda e: e.copy(self.stats2b[:], self.stats2[:]), inc=1)
        ev1 = S.ev("ac")
        self.ev_gn_done = ev1
        self._gn_ev1 = ev1

    def gn_p2(self, s):
        S = self.S
        ev2 = S.op("pe", lambda e: e.matmul(
            self.tot_ps, self.stats2b[:], self.ones_col[:], start=True, stop=True),
            waits=[self._gn_ev1, self.ev_onesc, self.bank_free[7]], inc=1)
        self._gn_ev3 = S.op("ac", lambda e: e.copy(self.tot_sb[:], self.tot_ps),
                            waits=[ev2], inc=1)

    def gn_p3(self, s):
        S = self.S
        ev4 = S.op("pe", lambda e: e.transpose(
            self.tr2_ps, self.tot_sb[:], self.ident[0:2, 0:2]),
            waits=[self._gn_ev3], inc=1)
        ev5 = S.op("ac", lambda e: e.copy(self.trow[:], self.tr2_ps),
                   waits=[ev4], inc=1)
        sc = self.sc_sb
        S.op("ac", lambda e: e.mul(sc[0:1, 0:1], self.trow[0:1, 0:1], 1.0 / GN_N),
             waits=[ev5])
        S.op("ac", lambda e: e.mul(sc[0:1, 1:2], self.trow[0:1, 1:2], -1.0 / GN_N))
        S.op("ac", lambda e: e.mul(sc[0:1, 2:3], sc[0:1, 0:1], sc[0:1, 0:1]))
        S.op("ac", lambda e: e.add(sc[0:1, 2:3], sc[0:1, 2:3], sc[0:1, 1:2]))
        S.op("ac", lambda e: e.activation(sc[0:1, 6:7], sc[0:1, 2:3], AF.Sqrt,
                                          scale=-1.0, bias=self.eps_col[0:1, :]),
             waits=[self.ev_eps], inc=1)
        ev_gstd = S.ev("ac")
        ev_grec = S.op("dv", lambda e: e.reciprocal(sc[0:1, 3:4], sc[0:1, 6:7]),
                       waits=[ev_gstd], inc=1)
        S.op("dv", lambda e: e.scalar_tensor_tensor(
            sc[0:1, 5:6], sc[0:1, 0:1], -1.0, sc[0:1, 3:4], ALU.mult, ALU.mult),
            force_waits=[ev_grec])
        S.op("dv", lambda e: e.tensor_copy(self.vals2[0:1, 0:1], sc[0:1, 3:4]))
        S.op("dv", lambda e: e.tensor_copy(self.vals2[0:1, 1:2], sc[0:1, 5:6]), inc=1)
        self._gn_ev9 = S.ev("dv")

    def gn_p4(self, s):
        S = self.S
        ev10 = S.op("pe", lambda e: e.matmul(
            self.bc2_ps, self.ones_row[:], self.vals2[:], start=True, stop=True),
            waits=[self._gn_ev9, self.ev_ones], inc=1)
        ev11 = S.op("ac", lambda e: e.copy(self.bc2[:], self.bc2_ps),
                    waits=[ev10], inc=1)
        self.bank_free[7] = ev11
        S.op("ac", lambda e, s=s: e.mul(
            self.acol[s][:], self.gnw_c[:], self.bc2[:, 0:1]),
            waits=[ev11, self.ev_const["gnw"]], inc=1)
        ev_a = S.ev("ac")
        S.op("dv", lambda e, s=s: e.scalar_tensor_tensor(
            self.bcol[s][:], self.gnw_c[:], self.bc2[:, 1:2], self.gnb_c[:],
            ALU.mult, ALU.add), waits=[ev11, self.ev_const["gnb"]], inc=1)
        self.ev_acols[s] = [ev_a, S.ev("dv")]

    def xs_store(self, s, duo, ev):
        ev_w = self.S.op(
            "gp", lambda e, s=s, duo=duo: e.dma_start(
                out=self.xs_d[s, :, :, duo * DUO:(duo + 1) * DUO].rearrange(
                    "d p t -> p d t"),
                in_=self.xs_duo[duo % 2].rearrange("p (d t) -> p d t", d=8)),
            waits=[ev], inc=16)
        self.ev_xsduo_w[(s, duo)] = ev_w
        self.ev_xsduo_dma[duo % 2] = ev_w

    def stats_and_xs(self, s, duo, ev_cols2):
        self.xsT_group(s, 2 * duo, 0, 4, ev_cols2[0])
        a2 = self.xsT_group(s, 2 * duo, 1, 5, ev_cols2[0])
        self.xsT_group(s, 2 * duo + 1, 0, 4, ev_cols2[1])
        b2 = self.xsT_group(s, 2 * duo + 1, 1, 5, ev_cols2[1])
        self.xs_store(s, duo, b2)
        return [a2, b2]

    def phase_a(self):
        lvl = STAGES.index(self.stage)
        S = self.S
        self.wgu_parts = []
        total = self.n_samples * self.n_duos

        def chain_full(s, c, dma=True):
            if dma:
                self.ln_stats_dma(s, c)
            return self.ln_stats_fin(s, c, self.ln_stats_pre(s, c))

        if not self.with_ret:
            for s in range(self.n_samples):
                ev_cols2 = [chain_full(s, 0, dma=(s > 0)),
                            chain_full(s, 1, dma=(s > 0))]
                evs_xs = self.stats_and_xs(s, 0, ev_cols2)
                for duo in range(self.n_duos - 1):
                    nxt = [chain_full(s, 2 * duo + 2), chain_full(s, 2 * duo + 3)]
                    evs_xs = self.stats_and_xs(s, duo + 1, nxt)
            return

        ev_cols2 = [chain_full(0, 0, dma=False), chain_full(0, 1, dma=False)]
        evs_xs = self.stats_and_xs(0, 0, ev_cols2)
        for g in range(total):
            s, d = divmod(g, self.n_duos)
            c0, c1 = 2 * d, 2 * d + 1
            box = {}
            drip = []
            if g + 1 < total:
                s2, d2 = divmod(g + 1, self.n_duos)
                self.ln_stats_dma(s2, 2 * d2)
                self.ln_stats_dma(s2, 2 * d2 + 1)
                drip = (self.ln_stats_pre_items(s2, 2 * d2, box, 'a')
                        + self.ln_stats_pre_items(s2, 2 * d2 + 1, box, 'b'))
            ev_qk = self.qk_proj(s, d, evs_xs[-1], drip=drip)
            self.prefetch_wgu(s, d)
            evs_xs_cur = evs_xs
            nxt_box = [None, None]
            if lvl >= 1:
                self.ktv_gkt_load(c0)
                k0 = [self.ktv_group(s, c0, 0, 0, evs_xs_cur[0]),
                      self.ktv_group(s, c0, 1, 1, evs_xs_cur[0])]
                v0 = [self.ktv_group(s, c0, 2, 2, evs_xs_cur[0]),
                      self.ktv_group(s, c0, 3, 3, evs_xs_cur[0])]
            nxt_info = None
            if g + 1 < total:
                nxt_info = (s2, d2,
                            [self.ln_stats_fin(s2, 2 * d2, box['a']),
                             self.ln_stats_fin(s2, 2 * d2 + 1, box['b'])])
            gn_pend = None
            if d == 0 and s > 0:
                gn_pend = s - 1
                self.gn_p1(s - 1)
                ev = S.op("dv", lambda e: e.memset(self.z_f32[:], 0.0),
                          waits=[self.ev_zuse], inc=1)
                self.ev_z0 = ev
                self.ev_zbf = None
                self.ev_zuse = None
            if lvl < 1:
                if nxt_info is not None:
                    s2, d2, cols = nxt_info
                    evs_xs = self.stats_and_xs(s2, d2, cols)
                continue
            if lvl < 2:
                self.ktv_gkt_load(c1)
                for grp, bank in ((0, 4), (1, 4), (2, 5), (3, 5)):
                    self.ktv_group(s, c1, grp, bank, evs_xs_cur[1])
                self.ev_ktv_use = [S.ev("pe"), S.ev("pe")]
                if nxt_info is not None:
                    s2, d2, cols = nxt_info
                    evs_xs = self.stats_and_xs(s2, d2, cols)
                continue

            def mk_xs(par, half, bank, nxt_info=nxt_info, nxt_box=nxt_box):
                if nxt_info is None:
                    return
                s2, d2, cols = nxt_info
                ev = self.xsT_group(s2, 2 * d2 + par, half, bank, cols[par])
                if half == 1:
                    nxt_box[par] = ev
                    if par == 1:
                        self.xs_store(s2, d2, ev)

            self.ktv_gkt_load(c1)
            ev_c1 = {}
            cbs = {
                3: lambda: ev_c1.__setitem__(
                    'k0', self.ktv_group(s, c1, 0, 4, evs_xs_cur[1])),
                4: lambda: ev_c1.__setitem__(
                    'v0', self.ktv_group(s, c1, 2, 5, evs_xs_cur[1])),
                5: lambda: mk_xs(0, 0, 4),
                6: lambda: mk_xs(0, 1, 5),
                7: lambda: ev_c1.__setitem__(
                    'k1', self.ktv_group(s, c1, 1, 4, evs_xs_cur[1])),
                'tail': lambda: ev_c1.__setitem__(
                    'v1', self.ktv_group(s, c1, 3, 5, evs_xs_cur[1])),
            }
            self.retention(s, c0, k0, v0, ev_qk, mid_cbs=cbs)
            self.ev_ktv_use[0] = S.ev("pe")
            cbs1 = {3: lambda: mk_xs(1, 0, 4),
                    4: lambda: mk_xs(1, 1, 5)}
            if gn_pend is not None:
                cbs1[5] = lambda s0=gn_pend: self.gn_p2(s0)
                cbs1[6] = lambda s0=gn_pend: self.gn_p3(s0)
                cbs1[7] = lambda s0=gn_pend: self.gn_p4(s0)
            self.retention(s, c1, [ev_c1['k0'], ev_c1['k1']],
                           [ev_c1['v0'], ev_c1['v1']], ev_qk,
                           mid_cbs=cbs1,
                           skip_zd=(c1 == NCH - 1))
            self.ev_ktv_use[1] = S.ev("pe")
            self.ev_xsduo_use[d % 2] = S.ev("pe")
            evs_xs = nxt_box
        self._gn_tail = None
        if lvl >= 4:
            if self.with_b and self.stage == "full":
                self.gn_p1(self.n_samples - 1)
                self._gn_tail = self.n_samples - 1
            else:
                self.gn_finalize(self.n_samples - 1)

    def prefetch_wgu(self, s, duo):
        i = s * self.n_duos + duo
        if i >= 8:
            return
        for arena, src in ((self.wg_a, self.wgT_in), (self.wu_a, self.wuT_in)):
            self.wgu_parts.append(self.S.op(
                "gp", lambda e, arena=arena, src=src, i=i: e.dma_start(
                    out=arena[:, i * HIDDEN:(i + 1) * HIDDEN],
                    in_=src[i * 128:(i + 1) * 128, :]),
                inc=16, lane=3))

    def phase_b(self):
        S = self.S
        evA = [S.ev("pe"), S.ev("dv"), S.ev("ac"), S.ev("gp"), S.ev("sy")]
        ev_wd = S.op("gp", lambda e: e.dma_start(
            out=self.wd_a[:].rearrange("p (a c) -> p a c", a=24),
            in_=self.wdT_in.rearrange("(a p) c -> p a c", p=128)),
            waits=[self.ev_wqkv_lastuse, evA], inc=16)

        ev_xs_use = [None, None]
        ev_r_use = [None, None]
        ev_x_use = None
        ev_out_use = [None, None]
        ev_hid_use = None
        ev_g_use = None
        gate_free = [self.bank_free[0], self.bank_free[2]]
        up_free = [self.bank_free[1], self.bank_free[3]]
        down_free = [self.bank_free[4 + i] for i in range(4)]
        ev_silu_use = None

        duos = [(BS - 1, NDUO - 1, True)] + [
            (s, u, False) for s in range(BS) for u in range(NDUO)
            if not (s == BS - 1 and u == NDUO - 1)]
        first_g_extra = [S.ev("gp"), S.ev("sy")]
        for idx, (s, u, direct) in enumerate(duos):
                par = idx % 2
                if direct:
                    ev_xs = None
                else:
                    ev_xs = S.op("sy", lambda e, s=s, u=u, par=par: e.dma_start(
                        out=self.xs_sup[par].rearrange("p (d t) -> p d t", d=8),
                        in_=self.xs_d[s, :, :, u * DUO:(u + 1) * DUO].rearrange(
                            "d p t -> p d t")),
                        waits=[self.ev_xsduo_w[(s, u)], ev_xs_use[par], evA],
                        inc=16)
                ev_r = S.op("sy", lambda e, s=s, u=u, par=par: e.dma_start(
                    out=self.r_sup[par].rearrange("p (d t) -> p d t", d=8),
                    in_=self.r_d[s, :, :, u * DUO:(u + 1) * DUO].rearrange(
                        "d p t -> p d t")),
                    waits=[self.ev_rchunk_w[(s, 2 * u)], self.ev_rchunk_w[(s, 2 * u + 1)],
                           ev_r_use[par], evA], inc=16)
                ev_x = S.op("sy", lambda e, s=s, u=u: e.dma_start(
                    out=self.x_sup.rearrange("p (a t) -> p a t", a=8),
                    in_=self.xT_in[s, :, u * DUO:(u + 1) * DUO].rearrange(
                        "(a p) t -> p a t", p=128)),
                    waits=[ev_x_use, evA], inc=16)

                for h_t in range(NHT):
                    hp = h_t % 2
                    gb = self.pb[0 + 2 * hp]
                    ub = self.pb[1 + 2 * hp]
                    ev_gmm = ev_umm = None
                    for d_i in range(8):
                        ev_gmm = S.op("pe", lambda e, h_t=h_t, d_i=d_i, gb=gb, par=par: e.matmul(
                            gb[:, 0:DUO], self.wg_t(d_i, h_t * 128, 128),
                            self.xs_sup[par][:, d_i * DUO:(d_i + 1) * DUO],
                            start=(d_i == 0), stop=(d_i == 7)),
                            waits=[ev_xs, self.wgu_parts, gate_free[hp]] if d_i == 0 else (),
                            inc=1 if d_i == 7 else 0)
                    for d_i in range(8):
                        ev_umm = S.op("pe", lambda e, h_t=h_t, d_i=d_i, ub=ub, par=par: e.matmul(
                            ub[:, 0:DUO], self.wu_t(d_i, h_t * 128, 128),
                            self.xs_sup[par][:, d_i * DUO:(d_i + 1) * DUO],
                            start=(d_i == 0), stop=(d_i == 7)),
                            waits=[self.wgu_parts, up_free[hp]] if d_i == 0 else (),
                            inc=1 if d_i == 7 else 0)
                    ev_sl = S.op("ac", lambda e, gb=gb: e.activation(
                        self.silu_b[:], gb[:, 0:DUO], AF.Silu, scale=QS),
                        waits=[ev_gmm, ev_silu_use], inc=1)
                    gate_free[hp] = ev_sl
                    ev_hd = S.op("dv", lambda e, h_t=h_t, ub=ub: e.scalar_tensor_tensor(
                        self.hid_b[:, h_t * DUO:(h_t + 1) * DUO], ub[:, 0:DUO],
                        QS, self.silu_b[:], ALU.mult, ALU.mult),
                        waits=[ev_umm, ev_sl, ev_hid_use], inc=1)
                    up_free[hp] = ev_hd
                    ev_silu_use = ev_hd
                    if h_t == NHT // 2 - 1:
                        ev_hid_half = ev_hd
                    if idx == 0 and self._gn_tail is not None:
                        if h_t == 4:
                            self.gn_p2(self._gn_tail)
                        elif h_t == 6:
                            self.gn_p3(self._gn_tail)
                        elif h_t == 8:
                            self.gn_p4(self._gn_tail)
                ev_hid = S.ev("dv")
                ev_hid_use = None

                ev_down_last = None
                for d_i in range(8):
                    db = self.pb[4 + d_i % 4]
                    for h_t in range(NHT):
                        if h_t == 0:
                            w = [ev_hid_half, ev_wd, down_free[d_i % 4]]
                        elif h_t == NHT // 2:
                            w = [ev_hid]
                        else:
                            w = ()
                        S.op("pe", lambda e, d_i=d_i, h_t=h_t, db=db: e.matmul(
                            db[:, 0:DUO], self.wd_t(h_t, d_i * 128, 128),
                            self.hid_b[:, h_t * DUO:(h_t + 1) * DUO],
                            start=(h_t == 0), stop=(h_t == NHT - 1)),
                            waits=w,
                            inc=1 if h_t == NHT - 1 else 0)
                    ev_dmm = S.ev("pe")
                    ev_down_last = ev_dmm
                    ev_g = S.op("ac", lambda e, d_i=d_i, db=db: e.activation(
                        self.g_sup[:, d_i * DUO:(d_i + 1) * DUO], db[:, 0:DUO],
                        AF.Silu, scale=QS),
                        waits=[ev_dmm, ev_g_use]
                        + (first_g_extra if idx == 0 and d_i == 0 else []),
                        inc=1)
                    down_free[d_i % 4] = ev_g

                    op = d_i % 2
                    s_loc = self.s_tile
                    S.op("dv", lambda e, s=s, d_i=d_i, par=par: e.tensor_scalar(
                        s_loc[:], self.r_sup[par][:, d_i * DUO:(d_i + 1) * DUO],
                        self.acol[s][:, d_i:d_i + 1], self.bcol[s][:, d_i:d_i + 1],
                        ALU.mult, ALU.add),
                        waits=[ev_r, self.ev_acols[s]])
                    S.op("dv", lambda e, d_i=d_i: e.tensor_mul(
                        self.sg_tile[:], s_loc[:],
                        self.g_sup[:, d_i * DUO:(d_i + 1) * DUO]),
                        waits=[ev_g])
                    ev_ob = S.op("dv", lambda e, d_i=d_i, op=op: e.tensor_add(
                        self.out_b[op][:], self.x_sup[:, d_i * DUO:(d_i + 1) * DUO],
                        self.sg_tile[:]),
                        waits=[ev_x, ev_out_use[op]], inc=1)
                    ev_wr = S.op("gp",
                                 lambda e, s=s, u=u, d_i=d_i, op=op: e.dma_start(
                        out=self.out_d[s, d_i * 128:(d_i + 1) * 128,
                                       u * DUO:(u + 1) * DUO],
                        in_=self.out_b[op][:]),
                        waits=[ev_ob], inc=16)
                    ev_out_use[op] = ev_wr
                ev_g_use = S.ev("dv")
                ev_hid_use = ev_down_last
                ev_xs_use[par] = ev_down_last
                ev_r_use[par] = S.ev("dv")
                ev_x_use = S.ev("dv")

    def build(self):
        self.setup()
        self.phase_a()
        if self.with_b and self.stage == "full":
            self.phase_b()
        nc = self.nc
        S = self.S
        import contextlib
        with contextlib.ExitStack() as stack:
            sems = {}
            for name in ["pe", "dv", "ac", "po"] + [f"{q}{l}" for q in ("sy", "gp")
                                                    for l in range(NLANE)]:
                sems[name] = stack.enter_context(nc.semaphore(f"s_{name}"))
            S.set_sems(sems)
            with nc.Block() as block:
                @block.sync
                def _(e):
                    S.emit("sy", e)
                    for l in range(NLANE):
                        e.wait_ge(sems[f"sy{l}"], S.lane_n["sy"][l])

                @block.gpsimd
                def _(e):
                    S.emit("gp", e)
                    for l in range(NLANE):
                        e.wait_ge(sems[f"gp{l}"], S.lane_n["gp"][l])

                @block.vector
                def _(e):
                    S.emit("dv", e)

                @block.scalar
                def _(e):
                    S.emit("ac", e)

                @block.tensor
                def _(e):
                    S.emit("pe", e)
        return nc


def host_inputs(x_full, past_kv, w_qkv, gn_w, gn_b, w_gate, w_up, w_down):
    x_full = np.ascontiguousarray(np.asarray(x_full, np.float32))
    w_qkv = np.ascontiguousarray(np.asarray(w_qkv, np.float32))
    past_kv = np.asarray(past_kv, np.float32)
    gn_w = np.asarray(gn_w, np.float32)
    gn_b = np.asarray(gn_b, np.float32)
    wgT = np.ascontiguousarray(np.asarray(w_gate, np.int32).T)
    wuT = np.ascontiguousarray(np.asarray(w_up, np.int32).T)
    wdT = np.ascontiguousarray(np.asarray(w_down, np.int32).T)

    gam = 1.0 - 2.0 ** (-5.0 - np.arange(NH, dtype=np.float64))
    i_loc = np.arange(CH, dtype=np.float64)
    gqf = np.zeros((128, NPAIR * CH), np.float64)
    gkf = np.zeros((128, NPAIR * CH), np.float64)
    for p in range(NPAIR):
        for half, h in ((0, 2 * p), (1, 2 * p + 1)):
            rows = slice(half * 64, half * 64 + 64)
            gqf[rows, p * CH:(p + 1) * CH] = gam[h] ** i_loc
            gkf[rows, p * CH:(p + 1) * CH] = gam[h] ** (-i_loc)
    g0q = np.zeros((128, NCH * NPAIR), np.float64)
    g0k = np.zeros((128, NCH * NPAIR), np.float64)
    for c in range(NCH):
        for p in range(NPAIR):
            for half, h in ((0, 2 * p), (1, 2 * p + 1)):
                rows = slice(half * 64, half * 64 + 64)
                g0q[rows, c * NPAIR + p] = gam[h] ** (128.0 * c)
                g0k[rows, c * NPAIR + p] = gam[h] ** (-128.0 * c)
    gkt = np.zeros((NCH, 128, D), np.float64)
    j = np.arange(CH, dtype=np.float64)
    for c in range(NCH):
        for h in range(NH):
            gkt[c, :, h * HD:(h + 1) * HD] = (gam[h] ** (-(128.0 * c + j)))[:, None]
    mask = np.zeros((128, 2 * CH), np.float32)
    tri = (np.arange(CH)[None, :] >= np.arange(CH)[:, None]).astype(np.float32)
    mask[:, 0:CH] = tri
    mask[:, CH:2 * CH] = tri

    pmat = np.zeros((128, NPAIR * HD), np.float32)
    for p in range(NPAIR):
        pmat[0:64, p * HD:(p + 1) * HD] = past_kv[2 * p]
        pmat[64:128, p * HD:(p + 1) * HD] = past_kv[2 * p + 1]
    gnw = np.zeros((128, NPAIR), np.float32)
    gnb = np.zeros((128, NPAIR), np.float32)
    for p in range(NPAIR):
        gnw[0:64, p] = gn_w[2 * p]
        gnw[64:128, p] = gn_w[2 * p + 1]
        gnb[0:64, p] = gn_b[2 * p]
        gnb[64:128, p] = gn_b[2 * p + 1]

    shared = dict(
        wqkv=w_qkv, wgT=wgT, wuT=wuT, wdT=wdT,
        pmat=pmat, gnw=gnw, gnb=gnb,
        gqf=gqf.astype(BFNP), gkf=gkf.astype(BFNP),
        g0q=g0q.astype(np.float32), g0k=g0k.astype(np.float32),
        gkt=gkt.astype(BFNP), mask=mask.astype(BFNP),
        ident=np.eye(128, dtype=np.float32),
    )
    in_maps = []
    for core in range(NCORE):
        xs = np.ascontiguousarray(x_full[core * BS:(core + 1) * BS])
        xT = np.ascontiguousarray(xs.transpose(0, 2, 1))
        m = dict(shared)
        m["x"] = xs
        m["xT"] = xT
        in_maps.append(m)
    return in_maps


def kernel(L0_attn_x, L0_attn_past_kv, L0_attn_W_qkv, L0_attn_gn_weight,
           L0_attn_gn_bias, L0_ffn_w_gate, L0_ffn_w_up, L0_ffn_w_down):
    in_maps = host_inputs(L0_attn_x, L0_attn_past_kv, L0_attn_W_qkv,
                          L0_attn_gn_weight, L0_attn_gn_bias,
                          L0_ffn_w_gate, L0_ffn_w_up, L0_ffn_w_down)
    nc = Builder().build()
    res = run_bass_kernel_spmd(nc, in_maps, core_ids=list(range(NCORE)))
    parts = [np.asarray(r["outT"], np.float32).transpose(0, 2, 1)
             for r in res.results]
    return np.ascontiguousarray(np.concatenate(parts, axis=0))
